# version 03
# baseline (speedup 1.0000x reference)
import math
import dataclasses
import numpy as np
from contextlib import ExitStack

import concourse.bass as bass
import concourse.mybir as mybir
from concourse.bass_utils import run_bass_kernel_spmd

F32 = mybir.dt.float32
BF16 = mybir.dt.bfloat16
AF = mybir.ActivationFunctionType
ALU = mybir.AluOpType
AX = mybir.AxisListType.X

P = 128
T = 4096
TH = 2048
C = 1024
NT = T // P
NTH = TH // P
NCI = C // P
NE = 32
NPAIR = 8
EPS = 1e-5
PI = math.pi

STAGES = "ABCD"
DEBUG_DUMPS = False
C_SUB = "full"

ENGINES = ["sync", "tensor", "vector", "scalar", "gpsimd"]


class Ctx:
    def __init__(self, nc):
        self.nc = nc
        self.sems = {}
        self.counts = {}
        self.cur = None
        self.eng = None
        self.es = None
        self._q0 = None

    def make_sem(self, name):
        h = self.es.enter_context(self.nc.semaphore(name))
        self.sems[name] = h
        self.counts[name] = 0

    def start(self, engname, eng):
        self.cur = engname
        self.eng = eng
        self._q0 = None
        for k in self.counts:
            self.counts[k] = 0

    def c(self, name):
        return self.counts[name]

    def q0row(self):
        if self._q0 is None:
            cid = self.eng.partition_id()
            self._q0 = (cid % 2) * (TH * C)
        return self._q0

    def op(self, engname, fn, waits=(), incs=()):
        if self.cur == engname:
            e = self.eng
            for s, v in waits:
                if v > 0:
                    e.wait_ge(self.sems[s], v)
            inst = fn(e)
            if incs:
                assert inst is not None, "op with incs must return an instruction"
                for s, n in incs:
                    inst.then_inc(self.sems[s], n)
        for s, n in incs:
            self.counts[s] += n

    def barrier(self):
        for en in ENGINES:
            self.op(en, lambda e: e.nop(), incs=[("bar", 1)])
        tgt = self.c("bar")
        for en in ENGINES:
            self.op(en, lambda e, t=tgt: (e.wait_ge(self.sems["bar"], t), None)[1])


def roff(ctx, ap):
    return dataclasses.replace(ap, offset=ap.offset + ctx.q0row())


SEMS = [
    "bar", "cst", "ld", "xin", "xin2", "lnv", "lns", "tr", "hfm",
    "wa", "prj", "prjv", "sc", "ms", "ex", "pv", "fin", "aux", "aux2",
    "aof", "yev", "res", "xst",
    "wl", "wf", "rt", "gu", "guv", "gus", "dn", "acc",
    "kv", "ksq", "qn", "ch", "chs", "l0", "out",
]


def build_nc():
    nc = bass.Bass("TRN2", target_bir_lowering=False, debug=False,
                   detect_race_conditions=False)
    ctx = Ctx(nc)
    es = ExitStack()
    ctx.es = es
    with es:
        for s in SEMS:
            ctx.make_sem(s)

        x0 = nc.dram_tensor("x0", [T, C], F32, kind="ExternalInput").ap()
        wattn = nc.dram_tensor("wattn", [C, 3 * C], F32, kind="ExternalInput").ap()
        wproj = nc.dram_tensor("wproj", [C, C], F32, kind="ExternalInput").ap()
        wqT = nc.dram_tensor("wqT", [C, C], F32, kind="ExternalInput").ap()
        wkT = nc.dram_tensor("wkT", [C, C], F32, kind="ExternalInput").ap()
        wvT = nc.dram_tensor("wvT", [C, C], F32, kind="ExternalInput").ap()
        woT = nc.dram_tensor("woT", [C, C], F32, kind="ExternalInput").ap()
        bias_h = nc.dram_tensor("bias_h", [P, 4, NCI], F32, kind="ExternalInput").ap()
        brow_h = nc.dram_tensor("brow_h", [4, C], F32, kind="ExternalInput").ap()
        gw_h = nc.dram_tensor("gw_h", [P, NCI, NE], F32, kind="ExternalInput").ap()
        guwT = nc.dram_tensor("guwT", [NE, C, C], F32, kind="ExternalInput").ap()
        dwT = nc.dram_tensor("dwT", [NE, 512, C], F32, kind="ExternalInput").ap()
        y_out = nc.dram_tensor("y", [TH, C], F32, kind="ExternalOutput").ap()
        xA = nc.dram_tensor("xA", [T, C], F32).ap()
        xB = nc.dram_tensor("xB", [T, C], F32).ap()
        xC = nc.dram_tensor("xC", [TH, C], F32).ap()
        xQ = nc.dram_tensor("xQ", [TH, C], F32).ap()
        if DEBUG_DUMPS:
            dbg_q = nc.dram_tensor("dbg_q", [P, 4096], BF16, kind="ExternalOutput").ap()
            dbg_k = nc.dram_tensor("dbg_k", [P, 4096], BF16, kind="ExternalOutput").ap()
            dbg_vi = nc.dram_tensor("dbg_vi", [P, 4160], BF16, kind="ExternalOutput").ap()
            dbg_sm = nc.dram_tensor("dbg_sm", [P, 16], F32, kind="ExternalOutput").ap()
            dbg_h = nc.dram_tensor("dbg_h", [P, 4096], BF16, kind="ExternalOutput").ap()
            dbg_ao = nc.dram_tensor("dbg_ao", [P, 4096], BF16, kind="ExternalOutput").ap()

        sb = lambda n, s, d=F32: es.enter_context(nc.sbuf_tensor(n, s, d))
        pool_x = sb("pool_x", [P, NCI * T], BF16)
        pool_y = sb("pool_y", [P, NTH * C], F32)
        pool_z = sb("pool_z", [P, 22528], BF16)
        msks = sb("msks", [P, 4096], BF16)
        x_tile = sb("x_tile", [P, C], F32)
        h_tm = sb("h_tm", [P, C], BF16)
        sqs = sb("sqs", [P, C], F32)
        rqb = sb("rqb", [P, TH], F32)
        rowslot = sb("rowslot", [1, 512], F32)
        rowslot2 = sb("rowslot2", [1, 512], F32)
        rlbuf = sb("rlbuf", [P, 512], F32)
        stat = sb("stat", [P, 2, 8], F32)
        consts = sb("consts", [P, 8], F32)
        bias_sb = sb("bias_sb", [P, 4, NCI], F32)
        gw_sb = sb("gw_sb", [P, NCI, NE], BF16)
        id_bf = sb("id_bf", [P, P], BF16)
        id_f32 = sb("id_f32", [P, P], F32)
        allones = sb("allones", [P, P], F32)
        ones1f = sb("ones1f", [P, 1], F32)
        ones2 = sb("ones2", [P, 2], BF16)
        onesr_b = sb("onesr_b", [P, 1], BF16)
        cmask = sb("cmask", [P, 2, P], F32)
        norms = sb("norms", [P, 96], F32)
        smalls = sb("smalls", [P, 16], F32)
        probs = sb("probs", [P, NE], F32)
        m8 = sb("m8", [P, 8], F32)
        pm = sb("pm", [P, NE], F32)

        ps = lambda n, s, d=F32: es.enter_context(nc.psum_tensor(n, s, d))
        psAB = ps("psAB", [P, 1024])
        psCD = ps("psCD", [P, 1024])
        psE = ps("psE", [P, 512])
        psF = ps("psF", [P, 512])
        psT = ps("psT", [P, 1024], BF16)
        psS = ps("psS", [P, 512])

        h_fm = pool_x[:].rearrange("p (ci t) -> p ci t", ci=NCI)
        K_fm = h_fm
        h4_fm = pool_x[:, 0:NCI * TH].rearrange("p (ci t) -> p ci t", ci=NCI)
        ao_fm = pool_y[:].bitcast(BF16).rearrange("p (ci t) -> p ci t", ci=NCI)
        acc_v = pool_y[:].rearrange("p (tt c) -> p tt c", tt=NTH)
        V_tm = pool_y[:].bitcast(BF16).rearrange("p (tt c) -> p tt c", tt=NT)

        zb = pool_z
        A_Qp = zb[:, 0:4096]
        A_Kp = zb[:, 4096:8192]
        A_VI = zb[:, 8192:8192 + 32 * 130].rearrange("p (tt v) -> p tt v", tt=NT)
        wpair = zb[:, 12352:15424].rearrange("p (ci w k) -> p ci w k", ci=NCI, w=3)
        A_sq = zb[:, 15424:15936]
        A_wproj = zb[:, 0:8192].rearrange("p (ci j) -> p ci j", ci=NCI)
        B_guw = zb[:, 0:16384].rearrange("p (b ci j) -> p b ci j", b=2, ci=NCI)
        B_dw = zb[:, 16384:20480].rearrange("p (ci j) -> p ci j", ci=4)
        B_sg = zb[:, 20480:22528].rearrange("p (g t) -> p g t", g=4)
        C_Q = zb[:, 0:16384].rearrange("p (ci t) -> p ci t", ci=NCI)
        C_wsl = zb[:, 16384:20480].rearrange("p (ci j) -> p ci j", ci=NCI)
        C_bvb = zb[:, 20480:22528].bitcast(F32)

        mask01 = msks[:, 0:2048].rearrange("p (m k) -> p m k", m=4)
        B_t = msks[:, 0:2048].rearrange("p (g t) -> p g t", g=4)
        comb = msks[:, 2048:4096].bitcast(F32).rearrange(
            "p (tt e) -> p tt e", tt=NT)
        C_aoqc = msks[:, 0:2048].rearrange("p (ci q) -> p ci q", ci=NCI)
        C_bob = msks[:, 2048:4096].bitcast(F32)

        A_pt = x_tile[:].bitcast(BF16)
        C_pt = sqs[:].bitcast(BF16)[:, 1024:1536]
        hq_v = rqb[:].bitcast(BF16).rearrange("p (ci t) -> p ci t", ci=NCI)
        lrow = rqb

        def emit_consts(c):
            def base(e):
                e.memset(consts[:, 0:1], EPS)
                e.memset(consts[:, 1:2], PI / 2)
                e.memset(consts[:, 2:3], 0.0)
                e.memset(allones[:], 1.0)
                e.memset(ones1f[:], 1.0)
                e.memset(onesr_b[:], 1.0)
                e.memset(ones2[0:64, 0:1], 1.0)
                e.memset(ones2[64:128, 0:1], 0.0)
                e.memset(ones2[0:64, 1:2], 0.0)
                e.memset(ones2[64:128, 1:2], 1.0)
                e.memset(cmask[0:64, :, :], 0.0)
                e.memset(cmask[64:128, 0, 0:64], 1.0)
                e.memset(cmask[64:128, 0, 64:128], 0.0)
                e.memset(cmask[64:128, 1, 0:64], 0.0)
                e.memset(cmask[64:128, 1, 64:128], 1.0)
                e.memset(A_VI[:, :, 64:65], 1.0)
                e.memset(id_bf[:], 0.0)
                e.memset(id_f32[:], 0.0)
                return e.memset(A_VI[:, :, 129:130], 1.0)
            c.op("gpsimd", base, incs=[("cst", 1)])
            c.op("gpsimd", lambda e: e.affine_select(
                out=id_bf[:], in_=id_bf[:], compare_op=ALU.not_equal,
                fill=1.0, base=0, pattern=[[-1, P]], channel_multiplier=1),
                waits=[("cst", c.c("cst"))], incs=[("cst", 1)])
            c.op("gpsimd", lambda e: e.affine_select(
                out=id_f32[:], in_=id_f32[:], compare_op=ALU.not_equal,
                fill=1.0, base=0, pattern=[[-1, P]], channel_multiplier=1),
                waits=[("cst", c.c("cst"))], incs=[("cst", 1)])
            for m in range(4):
                c.op("gpsimd", lambda e: e.memset(sqs[:, 0:512], 1.0),
                     waits=[("ms", c.c("ms"))], incs=[("cst", 1)])
                c.op("gpsimd", lambda e, m=m: e.affine_select(
                    out=sqs[:, 0:512], in_=sqs[:, 0:512], compare_op=ALU.is_ge,
                    fill=0.0, base=-128 * m, pattern=[[1, 512]],
                    channel_multiplier=-1),
                    waits=[("cst", c.c("cst"))], incs=[("cst", 1)])
                c.op("vector", lambda e, m=m: e.tensor_copy(mask01[:, m], sqs[:, 0:512]),
                     waits=[("cst", c.c("cst"))], incs=[("ms", 1)])
            c.op("sync", lambda e: e.dma_start(out=bias_sb[:], in_=bias_h),
                 incs=[("ld", 16)])
            c.op("gpsimd", lambda e: e.dma_start(out=gw_sb[:], in_=gw_h),
                 waits=[("ms", c.c("ms"))], incs=[("wl", 16)])

        def emit_ln(c, x_ap, par, xwaits, dest_fm_ap=None, pst=None):
            st = lambda i: stat[:, par, i:i + 1]
            c.op("vector", lambda e: e.reduce_sum(st(0), x_ap, axis=AX),
                 waits=list(xwaits) + [("lns", c.c("lns"))], incs=[("lnv", 1)])
            c.op("scalar", lambda e: e.activation(out=sqs[:], in_=x_ap,
                                                  func=AF.Square),
                 waits=list(xwaits) + [("lnv", c.c("lnv"))], incs=[("lns", 1)])
            c.op("vector", lambda e: e.tensor_scalar_mul(st(2), st(0), 1.0 / C),
                 waits=[("lnv", c.c("lnv"))], incs=[("lnv", 1)])
            c.op("vector", lambda e: e.reduce_sum(st(1), sqs[:], axis=AX),
                 waits=[("lns", c.c("lns"))], incs=[("lnv", 1)])
            c.op("vector", lambda e: e.tensor_scalar_mul(st(3), st(1), 1.0 / C),
                 waits=[("lnv", c.c("lnv"))], incs=[("lnv", 1)])
            c.op("vector", lambda e: e.tensor_mul(st(4), st(2), st(2)),
                 waits=[("lnv", c.c("lnv")), ("lns", c.c("lns"))], incs=[("lnv", 1)])
            c.op("vector", lambda e: e.tensor_sub(st(4), st(3), st(4)),
                 waits=[("lnv", c.c("lnv"))], incs=[("lnv", 1)])
            c.op("scalar", lambda e: e.activation(out=st(5), in_=st(4), func=AF.Sqrt,
                                                  bias=consts[:, 0:1]),
                 waits=[("lnv", c.c("lnv"))], incs=[("lns", 1)])
            c.op("vector", lambda e: e.reciprocal(st(6), st(5)),
                 waits=[("lns", c.c("lns"))], incs=[("lnv", 1)])
            c.op("vector", lambda e: e.tensor_scalar(
                out=h_tm[:], in0=x_ap, scalar1=st(2), scalar2=st(6),
                op0=ALU.subtract, op1=ALU.mult),
                waits=[("lnv", c.c("lnv"))], incs=[("lnv", 1)])
            if dest_fm_ap is not None:
                hs = c.c("lnv")
                p2 = pst if pst is not None else psT[:]
                for k in range(NCI):
                    c.op("tensor", lambda e, k=k, p2=p2: e.transpose(
                        p2[:, k * P:(k + 1) * P], h_tm[:, k * P:(k + 1) * P],
                        id_bf[:]),
                        waits=([("lnv", hs), ("hfm", c.c("hfm"))] if k == 0 else []),
                        incs=([("tr", 1)] if k == NCI - 1 else []))
                c.op("vector", lambda e, p2=p2: e.tensor_copy(dest_fm_ap, p2[:, 0:C]),
                     waits=[("tr", c.c("tr"))], incs=[("hfm", 1)])

        def phase_h1(c):
            for tt in range(NT):
                par = tt % 2
                c.op("sync", lambda e, tt=tt: e.dma_start(
                    out=x_tile[:], in_=x0[tt * P:(tt + 1) * P, :]),
                    waits=([("lnv", c.c("lnv")), ("lns", c.c("lns"))]
                           if tt > 0 else [("ms", c.c("ms"))]),
                    incs=[("xin", 16)])
                emit_ln(c, x_tile[:], par, [("xin", c.c("xin"))],
                        dest_fm_ap=h_fm[:, :, tt * P:(tt + 1) * P],
                        pst=(psCD[:].bitcast(BF16) if tt % 2 else None))

        def colmax_scalar(c, col_ap, k, dst_ap, wv):
            c.op("vector", lambda e: e.reduce_max(norms[0:k, 94:95], col_ap,
                                                  axis=AX),
                 waits=list(wv) + [("aux", c.c("aux"))], incs=[("fin", 1)])
            c.op("tensor", lambda e: e.transpose(psS[0:1, 256:256 + k],
                                                 norms[0:k, 94:95],
                                                 id_f32[0:k, 0:k]),
                 waits=[("fin", c.c("fin"))], incs=[("aux", 1)])
            c.op("vector", lambda e: e.reduce_max(dst_ap, psS[0:1, 256:256 + k],
                                                  axis=AX),
                 waits=[("aux", c.c("aux"))], incs=[("fin", 1)])

        def bcast_col(c, src11_ap, dst_col_ap, wv):
            c.op("tensor", lambda e: e.matmul(psS[:, 264:265], allones[0:1, :],
                                              src11_ap, start=True, stop=True),
                 waits=list(wv) + [("aux2", c.c("aux2"))], incs=[("aux", 1)])
            c.op("scalar", lambda e: e.copy(dst_col_ap, psS[:, 264:265]),
                 waits=[("aux", c.c("aux"))], incs=[("aux2", 1)])

        def stage_A(c):
            for pr in range(NPAIR):
                wsrc = wattn.rearrange("(ci p) n -> p ci n", p=P)
                for j, base in enumerate((0, C, 2 * C)):
                    c.op("gpsimd", lambda e, j=j, base=base, pr=pr: e.dma_start(
                        out=wpair[:, :, j, :],
                        in_=wsrc[:, :, base + pr * P: base + (pr + 1) * P]),
                        waits=([("pv", c.c("pv")), ("aux", c.c("aux"))]
                               if (pr > 0 and j == 0) else []),
                        incs=[("wa", 16)])
                wl = c.c("wa")
                hfm_all = c.c("hfm")
                for which, dst in ((0, A_Qp), (1, A_Kp)):
                    for tc in range(8):
                        pse = psE if (tc + which * 8) % 2 == 0 else psF
                        for ci in range(NCI):
                            c.op("tensor", lambda e, ci=ci, tc=tc, which=which, pse=pse: e.matmul(
                                pse[:], wpair[:, ci, which, :],
                                h_fm[:, ci, tc * 512:(tc + 1) * 512],
                                start=(ci == 0), stop=(ci == NCI - 1)),
                                waits=([("wa", wl), ("hfm", hfm_all),
                                        ("prj", max(0, c.c("prj") - 2)),
                                        ("prjv", max(0, c.c("prjv") - 1))]
                                       if ci == 0 else []),
                                incs=([("sc", 1)] if ci == NCI - 1 else []))
                        c.op("scalar", lambda e, tc=tc, dst=dst, pse=pse: e.activation(
                            out=dst[:, tc * 512:(tc + 1) * 512], in_=pse[:],
                            func=AF.Copy), waits=[("sc", c.c("sc"))],
                            incs=[("prj", 1)])
                        c.op("scalar", lambda e, pse=pse: e.activation(
                            out=A_sq[:, :], in_=pse[:], func=AF.Square),
                            waits=[("aux", c.c("aux"))], incs=[("prj", 1)])
                        c.op("tensor", lambda e: e.matmul(
                            psS[0:2, 0:512], ones2[:], A_sq[:, :],
                            start=True, stop=True),
                            waits=[("prj", c.c("prj")), ("prjv", c.c("prjv"))],
                            incs=[("aux", 1)])
                        slot = which * 8 + tc
                        c.op("vector", lambda e, slot=slot: e.reduce_max(
                            norms[0:2, 64 + slot:65 + slot], psS[0:2, 0:512], axis=AX),
                            waits=[("aux", c.c("aux"))], incs=[("prjv", 1)])
                c.op("vector", lambda e: e.reduce_max(norms[0:2, 80:81],
                                                      norms[0:2, 64:80], axis=AX),
                     waits=[("prjv", c.c("prjv"))], incs=[("fin", 1)])
                c.op("tensor", lambda e: e.transpose(psS[0:1, 256:258],
                                                     norms[0:2, 80:81],
                                                     id_f32[0:2, 0:2]),
                     waits=[("fin", c.c("fin"))], incs=[("aux", 1)])
                c.op("vector", lambda e: e.reduce_max(smalls[0:1, 0:1],
                                                      psS[0:1, 256:258], axis=AX),
                     waits=[("aux", c.c("aux"))], incs=[("fin", 1)])
                c.op("vector", lambda e: e.tensor_scalar_mul(smalls[0:1, 3:4],
                                                             smalls[0:1, 0:1],
                                                             -0.125),
                     waits=[("fin", c.c("fin"))], incs=[("fin", 1)])
                bcast_col(c, smalls[0:1, 3:4], smalls[:, 4:5],
                          [("fin", c.c("fin")), ("ex", c.c("ex"))])
                for tt in range(NT):
                    pse = psE if tt % 2 == 0 else psF
                    for ci in range(NCI):
                        c.op("tensor", lambda e, tt=tt, ci=ci, pse=pse: e.matmul(
                            pse[:, 0:128], h_fm[:, ci, tt * P:(tt + 1) * P],
                            wpair[:, ci, 2, :],
                            start=(ci == 0), stop=(ci == NCI - 1)),
                            waits=([("prjv", max(0, c.c("prjv") - 1)),
                                    ("prj", c.c("prj"))]
                                   if ci == 0 else []),
                            incs=([("sc", 1)] if ci == NCI - 1 else []))
                    c.op("vector", lambda e, tt=tt, pse=pse: e.tensor_copy(
                        A_VI[:, tt, 0:64], pse[:, 0:64]),
                        waits=[("sc", c.c("sc"))], incs=[])
                    c.op("vector", lambda e, tt=tt, pse=pse: e.tensor_copy(
                        A_VI[:, tt, 65:129], pse[:, 64:128]),
                        incs=[("prjv", 1)])
                prod_v = c.c("prjv")
                if DEBUG_DUMPS and pr == 0:
                    c.op("sync", lambda e: e.dma_start(out=dbg_q, in_=A_Qp),
                         waits=[("prjv", prod_v), ("prj", c.c("prj")),
                                ("aux2", c.c("aux2"))],
                         incs=[("out", 16)])
                    c.op("sync", lambda e: e.dma_start(out=dbg_k, in_=A_Kp),
                         incs=[("out", 16)])
                    c.op("sync", lambda e: e.dma_start(
                        out=dbg_vi, in_=A_VI.rearrange("p a b -> p (a b)")),
                        incs=[("out", 16)])
                    c.op("sync", lambda e: e.dma_start(out=dbg_sm, in_=smalls[:]),
                         incs=[("out", 16)])
                    c.op("sync", lambda e: e.dma_start(
                        out=dbg_h, in_=h_fm[:, 0, :]),
                        incs=[("out", 16)])
                for qc in range(8):
                    q_sl = slice(qc * 512, (qc + 1) * 512)
                    nkc = 4 * (qc + 1)
                    for kc in range(nkc):
                        sc_ps = psAB if kc % 2 == 0 else psCD
                        diag = kc >= 4 * qc
                        first = (qc == 0 and kc == 0)
                        c.op("tensor", lambda e, kc=kc, sc_ps=sc_ps, q_sl=q_sl: e.matmul(
                            sc_ps[:, 0:512], A_Kp[0:64, kc * P:(kc + 1) * P],
                            A_Qp[0:64, q_sl], start=True, stop=True),
                            waits=([("ex", max(0, c.c("ex") - 1))] +
                                   ([("prjv", prod_v), ("aux2", c.c("aux2")),
                                     ("xin", c.c("xin")), ("lns", c.c("lns")),
                                     ("lnv", c.c("lnv"))]
                                    if first else [])),
                            incs=[])
                        c.op("tensor", lambda e, kc=kc, sc_ps=sc_ps, q_sl=q_sl: e.matmul(
                            sc_ps[:, 512:1024], A_Kp[64:128, kc * P:(kc + 1) * P],
                            A_Qp[64:128, q_sl], start=True, stop=True,
                            tile_position=(64, 0)),
                            incs=[("sc", 1)])
                        pt = A_pt[:, (kc % 2) * 1024:(kc % 2 + 1) * 1024]
                        c.op("scalar", lambda e, sc_ps=sc_ps, pt=pt: e.activation(
                            out=pt, in_=sc_ps[:], func=AF.Exp, scale=0.125,
                            bias=smalls[:, 4:5]),
                            waits=[("sc", c.c("sc")),
                                   ("pv", max(0, c.c("pv") - 1)),
                                   ("ms", c.c("ms"))],
                            incs=[("ex", 1)])
                        if diag:
                            m = kc - 4 * qc
                            c.op("vector", lambda e, m=m, pt=pt: e.tensor_mul(
                                pt[:, 0:512], pt[:, 0:512], mask01[:, m]),
                                waits=[("ex", c.c("ex"))], incs=[])
                            c.op("vector", lambda e, m=m, pt=pt: e.tensor_mul(
                                pt[:, 512:1024], pt[:, 512:1024], mask01[:, m]),
                                incs=[("ms", 1)])
                        pvw = ([("ms", c.c("ms"))] if diag else
                               [("ex", c.c("ex"))])
                        c.op("tensor", lambda e, kc=kc, pt=pt, nkc=nkc: e.matmul(
                            psE[0:65, :], A_VI[:, kc, 0:65], pt[:, 0:512],
                            start=(kc == 0), stop=(kc == nkc - 1)),
                            waits=pvw + ([("fin", c.c("fin")),
                                          ("aof", c.c("aof"))] if kc == 0 else []),
                            incs=[])
                        c.op("tensor", lambda e, kc=kc, pt=pt, nkc=nkc: e.matmul(
                            psF[0:65, :], A_VI[:, kc, 65:130], pt[:, 512:1024],
                            start=(kc == 0), stop=(kc == nkc - 1)),
                            incs=[("pv", 1)])
                    c.op("vector", lambda e: e.reciprocal(lrow[64:65, 0:512],
                                                          psE[64:65, :]),
                         waits=[("pv", c.c("pv"))], incs=[("fin", 1)])
                    c.op("vector", lambda e: e.reciprocal(lrow[64:65, 512:1024],
                                                          psF[64:65, :]),
                         incs=[("fin", 1)])
                    c.op("tensor", lambda e: e.matmul(psS[:, 0:512],
                                                      cmask[64:65, 0, :],
                                                      lrow[64:65, 0:512],
                                                      start=True, stop=False),
                         waits=[("fin", c.c("fin")), ("aux2", c.c("aux2"))],
                         incs=[])
                    c.op("tensor", lambda e: e.matmul(psS[:, 0:512],
                                                      cmask[64:65, 1, :],
                                                      lrow[64:65, 512:1024],
                                                      start=False, stop=True),
                         incs=[("aux", 1)])
                    c.op("scalar", lambda e: e.copy(rlbuf[:], psS[:, 0:512]),
                         waits=[("aux", c.c("aux"))], incs=[("aux2", 1)])
                    c.op("vector", lambda e, pr=pr, q_sl=q_sl: e.tensor_tensor(
                        ao_fm[0:64, pr, q_sl], psE[0:64, :], rlbuf[0:64, :],
                        ALU.mult),
                        waits=[("aux2", c.c("aux2"))], incs=[("fin", 1)])
                    c.op("vector", lambda e, pr=pr, q_sl=q_sl: e.tensor_tensor(
                        ao_fm[64:128, pr, q_sl], psF[0:64, :], rlbuf[64:128, :],
                        ALU.mult),
                        incs=[("aof", 1)])
            aof_all = c.c("aof")
            if DEBUG_DUMPS:
                c.op("sync", lambda e: e.dma_start(
                    out=dbg_ao, in_=ao_fm[:, 0, :]),
                    waits=[("aof", aof_all)], incs=[("out", 16)])
            c.op("gpsimd", lambda e: e.dma_start(
                out=A_wproj[:], in_=wproj.rearrange("(ci p) j -> p ci j", p=P)),
                waits=[("pv", c.c("pv")), ("aux", c.c("aux"))], incs=[("wa", 16)])
            wpl = c.c("wa")
            for tt in range(NT):
                par = tt % 2
                c.op("sync", lambda e, tt=tt: e.dma_start(
                    out=x_tile[:], in_=x0[tt * P:(tt + 1) * P, :]),
                    waits=[("aof", aof_all), ("res", c.c("res")),
                           ("lnv", c.c("lnv")), ("lns", c.c("lns")),
                           ("xst", c.c("xst")), ("ex", c.c("ex")),
                           ("ms", c.c("ms"))],
                    incs=[("xin", 16)])
                xinw = c.c("xin")
                for jh in range(2):
                    pse = psE if jh == 0 else psF
                    for ci in range(NCI):
                        c.op("tensor", lambda e, tt=tt, jh=jh, ci=ci, pse=pse: e.matmul(
                            pse[:], ao_fm[:, ci, tt * P:(tt + 1) * P],
                            A_wproj[:, ci, jh * 512:(jh + 1) * 512],
                            start=(ci == 0), stop=(ci == NCI - 1)),
                            waits=([("aof", aof_all), ("wa", wpl),
                                    ("res", max(0, c.c("res") - 1)),
                                    ("fin", c.c("fin"))]
                                   if ci == 0 else []),
                            incs=([("yev", 1)] if ci == NCI - 1 else []))
                    c.op("vector", lambda e, jh=jh, pse=pse: e.tensor_add(
                        x_tile[:, jh * 512:(jh + 1) * 512],
                        x_tile[:, jh * 512:(jh + 1) * 512], pse[:]),
                        waits=[("yev", c.c("yev")), ("xin", xinw)],
                        incs=[("res", 1)])
                c.op("sync", lambda e, tt=tt: e.dma_start(
                    out=xA[tt * P:(tt + 1) * P, :], in_=x_tile[:]),
                    waits=[("res", c.c("res"))], incs=[("xst", 16)])
                if "B" in STAGES:
                    emit_ln(c, x_tile[:], par, [("res", c.c("res"))],
                            dest_fm_ap=h_fm[:, :, tt * P:(tt + 1) * P],
                            pst=(psCD[:].bitcast(BF16) if tt % 2 else None))

        def stage_ffn(c, which):
            full = which == "B"
            ntile = NT if full else NTH
            hsrc = h_fm if full else h4_fm
            xsrc = xA if full else xC
            if not full:
                for tt in range(NTH):
                    par = tt % 2
                    c.op("sync", lambda e, tt=tt: e.dma_start(
                        out=x_tile[:], in_=xC[tt * P:(tt + 1) * P, :]),
                        waits=[("lnv", c.c("lnv")), ("lns", c.c("lns"))],
                        incs=[("xin", 16)])
                    emit_ln(c, x_tile[:], par, [("xin", c.c("xin"))],
                            dest_fm_ap=h4_fm[:, :, tt * P:(tt + 1) * P],
                            pst=(psCD[:].bitcast(BF16) if tt % 2 else None))
            hfm_need = c.c("hfm")
            for tt in range(ntile):
                for ci in range(NCI):
                    c.op("tensor", lambda e, tt=tt, ci=ci: e.matmul(
                        psS[:, 0:NE], hsrc[:, ci, tt * P:(tt + 1) * P],
                        gw_sb[:, ci, :],
                        start=(ci == 0), stop=(ci == NCI - 1)),
                        waits=([("hfm", hfm_need), ("wl", 16),
                                ("rt", c.c("rt"))] if ci == 0 else []),
                        incs=([("sc", 1)] if ci == NCI - 1 else []))
                c.op("vector", lambda e: e.reduce_max(stat[:, 0, 0:1], psS[:, 0:NE],
                                                      axis=AX),
                     waits=[("sc", c.c("sc"))], incs=[("rt", 1)])
                c.op("vector", lambda e: e.tensor_scalar_mul(stat[:, 0, 1:2],
                                                             stat[:, 0, 0:1], -1.0),
                     waits=[("rt", c.c("rt"))], incs=[("rt", 1)])
                c.op("scalar", lambda e: e.activation(out=probs[:], in_=psS[:, 0:NE],
                                                      func=AF.Exp,
                                                      bias=stat[:, 0, 1:2]),
                     waits=[("rt", c.c("rt"))], incs=[("rt", 1)])
                c.op("vector", lambda e: e.max(out=m8[:], in_=probs[:]),
                     waits=[("rt", c.c("rt"))], incs=[("rt", 1)])
                c.op("vector", lambda e: e.tensor_scalar(pm[:], probs[:],
                                                         m8[:, 7:8], None,
                                                         op0=ALU.is_ge),
                     waits=[("rt", c.c("rt"))], incs=[("rt", 1)])
                c.op("vector", lambda e: e.tensor_mul(pm[:], pm[:], probs[:]),
                     waits=[("rt", c.c("rt"))], incs=[("rt", 1)])
                c.op("vector", lambda e: e.reduce_sum(stat[:, 0, 2:3], pm[:], axis=AX),
                     waits=[("rt", c.c("rt"))], incs=[("rt", 1)])
                c.op("vector", lambda e: e.reciprocal(stat[:, 0, 3:4], stat[:, 0, 2:3]),
                     waits=[("rt", c.c("rt"))], incs=[("rt", 1)])
                c.op("vector", lambda e, tt=tt: e.tensor_scalar_mul(
                    comb[:, tt, :], pm[:], stat[:, 0, 3:4]),
                    waits=[("rt", c.c("rt"))], incs=[("rt", 1)])
            rt_done = c.c("rt")
            nhalf = 2 if full else 1
            for hf in range(nhalf):
                tok0 = hf * TH
                c.op("sync", lambda e, tok0=tok0: e.dma_start(
                    out=acc_v[:], in_=xsrc[tok0:tok0 + TH, :].rearrange(
                        "(tt p) c -> p tt c", p=P)),
                    waits=[("acc", c.c("acc")), ("xst", c.c("xst"))],
                    incs=[("xin2", 16)])
                accw = c.c("xin2")
                for e_i in range(NE):
                    par = e_i % 2
                    c.op("gpsimd", lambda e, e_i=e_i, par=par: e.dma_start(
                        out=B_guw[:, par], in_=guwT[e_i].rearrange(
                            "(ci p) j -> p ci j", p=P)),
                        waits=[("wf", max(0, c.c("wf") - 2))],
                        incs=[("wl", 16)])
                    guw_rdy = c.c("wl")
                    c.op("gpsimd", lambda e, e_i=e_i: e.dma_start(
                        out=B_dw[:], in_=dwT[e_i].rearrange(
                            "(ci p) j -> p ci j", p=P)),
                        waits=[("dn", c.c("dn"))],
                        incs=[("wl", 16)])
                    dw_rdy = c.c("wl")
                    for tc in range(4):
                        tsl = slice(tok0 + tc * 512, tok0 + (tc + 1) * 512)
                        for guj in range(8):
                            bank = [psAB[:, 0:512], psAB[:, 512:1024],
                                    psCD[:, 0:512], psCD[:, 512:1024]][guj % 4]
                            bw = ([("guv", c.c("guv"))] if guj < 4
                                  else [("gus", c.c("gus"))])
                            for ci in range(NCI):
                                c.op("tensor", lambda e, guj=guj, ci=ci, bank=bank, tsl=tsl, par=par: e.matmul(
                                    bank, B_guw[:, par, ci, guj * P:(guj + 1) * P],
                                    hsrc[:, ci, tsl],
                                    start=(ci == 0), stop=(ci == NCI - 1)),
                                    waits=(([("wl", guw_rdy)] + bw) if ci == 0 else []),
                                    incs=([("gu", 1)] if ci == NCI - 1 else []))
                            if guj < 4:
                                c.op("scalar", lambda e, guj=guj, bank=bank: e.activation(
                                    out=B_sg[:, guj, :], in_=bank, func=AF.Silu),
                                    waits=[("gu", c.c("gu")),
                                           ("guv", c.c("guv"))],
                                    incs=[("gus", 1)])
                            else:
                                c.op("vector", lambda e, guj=guj, bank=bank: e.tensor_tensor(
                                    out=B_t[:, guj - 4, :], in0=bank,
                                    in1=B_sg[:, guj - 4, :], op=ALU.mult),
                                    waits=[("gu", c.c("gu")),
                                           ("gus", c.c("gus")),
                                           ("dn", c.c("dn")),
                                           ("ms", c.c("ms"))],
                                    incs=[("guv", 1)])
                        guv_now = c.c("guv")
                        for st4 in range(4):
                            gt = tok0 // P + tc * 4 + st4
                            for jh in range(2):
                                pse = psE if (st4 * 2 + jh) % 2 == 0 else psF
                                for ich in range(4):
                                    c.op("tensor", lambda e, st4=st4, jh=jh, ich=ich, pse=pse: e.matmul(
                                        pse[:], B_t[:, ich, st4 * P:(st4 + 1) * P],
                                        B_dw[:, ich, jh * 512:(jh + 1) * 512],
                                        start=(ich == 0), stop=(ich == 3)),
                                        waits=([("guv", guv_now), ("wl", dw_rdy),
                                                ("acc", max(0, c.c("acc") - 1))]
                                               if ich == 0 else []),
                                        incs=([("dn", 1)] if ich == 3 else []))
                                c.op("vector", lambda e, gt=gt, jh=jh, pse=pse, e_i=e_i, hf=hf: e.scalar_tensor_tensor(
                                    out=acc_v[:, gt - hf * NTH, jh * 512:(jh + 1) * 512],
                                    in0=pse[:], scalar=comb[:, gt, e_i:e_i + 1],
                                    in1=acc_v[:, gt - hf * NTH, jh * 512:(jh + 1) * 512],
                                    op0=ALU.mult, op1=ALU.add),
                                    waits=[("dn", c.c("dn")), ("xin2", accw),
                                           ("rt", rt_done)],
                                    incs=[("acc", 1)])
                    c.op("tensor", lambda e: e.nop(), incs=[("wf", 1)])
                acc_done = c.c("acc")
                if full:
                    for tt in range(NTH):
                        c.op("sync", lambda e, tt=tt, tok0=tok0: e.dma_start(
                            out=xB[tok0 + tt * P: tok0 + (tt + 1) * P, :],
                            in_=acc_v[:, tt, :]),
                            waits=[("acc", acc_done)], incs=[("xst", 16)])
                else:
                    for tt in range(NTH):
                        c.op("sync", lambda e, tt=tt: e.dma_start(
                            out=y_out[tt * P:(tt + 1) * P, :], in_=acc_v[:, tt, :]),
                            waits=[("acc", acc_done)], incs=[("out", 16)])

        def stage_C(c):
            c.op("sync", lambda e: e.dma_start(
                out=xQ[:, :], in_=roff(c, xB[0:TH, :])),
                waits=[("xst", c.c("xst"))], incs=[("xin2", 16)])
            xq_rdy = c.c("xin2")
            if C_SUB == "xq":
                return
            _njh = 1 if C_SUB.startswith("c1a") or C_SUB == "c1b" else 2
            _nwv = 1 if C_SUB.startswith("c1a") else 2
            for jh in range(_njh):
                for wv_pass in range(_nwv):
                    wt = wkT if wv_pass == 0 else wvT
                    c.op("gpsimd", lambda e, wt=wt, jh=jh: e.dma_start(
                        out=C_wsl[:, :, :],
                        in_=wt.rearrange("(ci p) j -> p ci j", p=P)
                        [:, :, jh * 512:(jh + 1) * 512]),
                        waits=[("sc", c.c("sc"))],
                        incs=[("wa", 16)])
                    wrdy = c.c("wa")
                    if wv_pass == 1:
                        c.op("sync", lambda e, jh=jh: e.dma_start(
                            out=rowslot[:], in_=brow_h[2:3, jh * 512:(jh + 1) * 512]),
                            waits=[("aux", c.c("aux"))], incs=[("ld", 16)])
                        c.op("tensor", lambda e: e.matmul(
                            psS[:, 0:512], allones[0:1, :], rowslot[:],
                            start=True, stop=True),
                            waits=[("ld", c.c("ld")), ("aux2", c.c("aux2")),
                                   ("kv", c.c("kv")), ("ch", c.c("ch")),
                                   ("fin", c.c("fin"))],
                            incs=[("aux", 1)])
                        c.op("scalar", lambda e, jh=jh: e.copy(
                            C_bvb[:, jh * 512:(jh + 1) * 512], psS[:, 0:512]),
                            waits=[("aux", c.c("aux"))], incs=[("aux2", 1)])
                    bvb_rdy = c.c("aux2")
                    for tc in range(8):
                        for st4 in range(4):
                            tt = tc * 4 + st4
                            par = tt % 2
                            c.op("sync", lambda e, tt=tt: e.dma_start(
                                out=x_tile[:], in_=xB[tt * P:(tt + 1) * P, :]),
                                waits=[("lnv", c.c("lnv")), ("lns", c.c("lns")),
                                       ("xst", c.c("xst"))],
                                incs=[("xin", 16)])
                            emit_ln(c, x_tile[:], par, [("xin", c.c("xin"))],
                                    dest_fm_ap=hq_v[:, :, st4 * P:(st4 + 1) * P])
                        hq_rdy = c.c("hfm")
                        if C_SUB == "c1a0":
                            continue
                        if wv_pass == 0:
                            for jj in range(4):
                                jch = jh * 4 + jj
                                pse = psE if jj % 2 == 0 else psF
                                for ci in range(NCI):
                                    c.op("tensor", lambda e, jj=jj, ci=ci, pse=pse: e.matmul(
                                        pse[:], C_wsl[:, ci, jj * P:(jj + 1) * P],
                                        hq_v[:, ci, :],
                                        start=(ci == 0), stop=(ci == NCI - 1)),
                                        waits=([("hfm", hq_rdy), ("wa", wrdy),
                                                ("kv", max(0, c.c("kv") - 1)),
                                                ("ksq", max(0, c.c("ksq") - 1))]
                                               if ci == 0 else []),
                                        incs=([("sc", 1)] if ci == NCI - 1 else []))
                                c.op("vector", lambda e, jch=jch, tc=tc, pse=pse: e.tensor_scalar(
                                    out=K_fm[:, jch, tc * 512:(tc + 1) * 512],
                                    in0=pse[:], scalar1=bias_sb[:, 1, jch:jch + 1],
                                    scalar2=None, op0=ALU.add),
                                    waits=[("sc", c.c("sc")), ("ld", 16)],
                                    incs=[("kv", 1)])
                                if C_SUB == "c1a1":
                                    continue
                                c.op("scalar", lambda e, jch=jch, tc=tc: e.activation(
                                    out=rlbuf[:].bitcast(BF16)[:, 0:512],
                                    in_=K_fm[:, jch, tc * 512:(tc + 1) * 512],
                                    func=AF.Square),
                                    waits=[("qn", c.c("qn")), ("kv", c.c("kv")),
                                           ("ld", 16)],
                                    incs=[("ksq", 1)])
                                if C_SUB == "c1a2s":
                                    c.op("scalar", lambda e: e.nop(),
                                         waits=[("ksq", c.c("ksq"))],
                                         incs=[("qn", 1)])
                                else:
                                    c.op("tensor", lambda e, jj=jj: e.matmul(
                                        psS[0:1, 0:512], onesr_b[:],
                                        rlbuf[:].bitcast(BF16)[:, 0:512],
                                        start=(jj == 0), stop=(jj == 3)),
                                        waits=[("ksq", c.c("ksq"))] +
                                              ([("ch", c.c("ch")), ("fin", c.c("fin"))]
                                               if jj == 0 else []),
                                        incs=[("qn", 1)])
                            if C_SUB in ("c1a1", "c1a2", "c1a2s"):
                                continue
                            c.op("vector", lambda e: e.tensor_copy(
                                rowslot2[:], psS[0:1, 0:512]),
                                waits=[("qn", c.c("qn"))], incs=[("ch", 1)])
                            for q4 in range(4):
                                c.op("tensor", lambda e, q4=q4: e.transpose(
                                    psS[:, 256:257],
                                    rowslot2[0:1, q4 * P:(q4 + 1) * P],
                                    id_f32[0:1, 0:1]),
                                    waits=[("ch", c.c("ch")), ("fin", c.c("fin"))],
                                    incs=[("aux", 1)])
                                col = tc * 4 + q4
                                if jh == 0:
                                    c.op("vector", lambda e, col=col: e.tensor_copy(
                                        norms[:, col:col + 1], psS[:, 256:257]),
                                        waits=[("aux", c.c("aux"))], incs=[("fin", 1)])
                                else:
                                    c.op("vector", lambda e, col=col: e.tensor_add(
                                        norms[:, col:col + 1], norms[:, col:col + 1],
                                        psS[:, 256:257]),
                                        waits=[("aux", c.c("aux"))], incs=[("fin", 1)])
                        else:
                            for st4 in range(4):
                                tt = tc * 4 + st4
                                pse = psE if st4 % 2 == 0 else psF
                                for ci in range(NCI):
                                    c.op("tensor", lambda e, st4=st4, ci=ci, pse=pse: e.matmul(
                                        pse[:], hq_v[:, ci, st4 * P:(st4 + 1) * P],
                                        C_wsl[:, ci, :],
                                        start=(ci == 0), stop=(ci == NCI - 1)),
                                        waits=([("hfm", hq_rdy), ("wa", wrdy),
                                                ("kv", max(0, c.c("kv") - 1))]
                                               if ci == 0 else []),
                                        incs=([("sc", 1)] if ci == NCI - 1 else []))
                                c.op("vector", lambda e, tt=tt, jh=jh, pse=pse: e.tensor_add(
                                    V_tm[:, tt, jh * 512:(jh + 1) * 512], pse[:],
                                    C_bvb[:, jh * 512:(jh + 1) * 512]),
                                    waits=[("sc", c.c("sc")), ("aux2", bvb_rdy)],
                                    incs=[("kv", 1)])
            if C_SUB.startswith("c1"):
                return
            c.op("scalar", lambda e: e.activation(out=norms[:, 32:64],
                                                  in_=norms[:, 0:32], func=AF.Sqrt,
                                                  bias=consts[:, 2:3]),
                 waits=[("fin", c.c("fin"))], incs=[("chs", 1)])
            c.op("vector", lambda e: e.tensor_scalar_add(norms[:, 32:64],
                                                         norms[:, 32:64], 1e-8),
                 waits=[("chs", c.c("chs"))], incs=[("ch", 1)])
            c.op("vector", lambda e: e.reciprocal(norms[:, 32:64], norms[:, 32:64]),
                 waits=[("ch", c.c("ch"))], incs=[("ch", 1)])
            c.op("vector", lambda e: e.tensor_scalar_mul(norms[:, 32:64],
                                                         norms[:, 32:64], PI),
                 waits=[("ch", c.c("ch"))], incs=[("ch", 1)])
            colmax_scalar(c, norms[:, 0:32], P, smalls[0:1, 8:9],
                          [("ch", c.c("ch"))])
            for jh in range(2):
                c.op("gpsimd", lambda e, jh=jh: e.dma_start(
                    out=C_wsl[:, :, :],
                    in_=wqT.rearrange("(ci p) j -> p ci j", p=P)
                    [:, :, jh * 512:(jh + 1) * 512]),
                    waits=[("sc", c.c("sc"))], incs=[("wa", 16)])
                wq_rdy = c.c("wa")
                for tc in range(4):
                    for st4 in range(4):
                        tt = tc * 4 + st4
                        par = tt % 2
                        c.op("sync", lambda e, tt=tt: e.dma_start(
                            out=x_tile[:],
                            in_=xQ[tt * P:(tt + 1) * P, :]),
                            waits=[("lnv", c.c("lnv")), ("lns", c.c("lns")),
                                   ("xin2", xq_rdy)],
                            incs=[("xin", 16)])
                        emit_ln(c, x_tile[:], par, [("xin", c.c("xin"))],
                                dest_fm_ap=hq_v[:, :, st4 * P:(st4 + 1) * P])
                    hq_rdy = c.c("hfm")
                    for jj in range(4):
                        jch = jh * 4 + jj
                        pse = psE if jj % 2 == 0 else psF
                        for ci in range(NCI):
                            c.op("tensor", lambda e, jj=jj, ci=ci, pse=pse: e.matmul(
                                pse[:], C_wsl[:, ci, jj * P:(jj + 1) * P],
                                hq_v[:, ci, :],
                                start=(ci == 0), stop=(ci == NCI - 1)),
                                waits=([("hfm", hq_rdy), ("wa", wq_rdy),
                                        ("kv", max(0, c.c("kv") - 1)),
                                        ("ksq", max(0, c.c("ksq") - 1))]
                                       if ci == 0 else []),
                                incs=([("sc", 1)] if ci == NCI - 1 else []))
                        c.op("vector", lambda e, jch=jch, tc=tc, pse=pse: e.tensor_scalar(
                            out=C_Q[:, jch, tc * 512:(tc + 1) * 512],
                            in0=pse[:], scalar1=bias_sb[:, 0, jch:jch + 1],
                            scalar2=None, op0=ALU.add),
                            waits=[("sc", c.c("sc"))], incs=[("kv", 1)])
                        c.op("scalar", lambda e, jch=jch, tc=tc: e.activation(
                            out=rlbuf[:].bitcast(BF16)[:, 0:512],
                            in_=C_Q[:, jch, tc * 512:(tc + 1) * 512],
                            func=AF.Square),
                            waits=[("qn", c.c("qn")), ("kv", c.c("kv")),
                                   ("ld", 16)],
                            incs=[("ksq", 1)])
                        c.op("tensor", lambda e, jj=jj: e.matmul(
                            psS[0:1, 0:512], onesr_b[:],
                            rlbuf[:].bitcast(BF16)[:, 0:512],
                            start=(jj == 0), stop=(jj == 3)),
                            waits=[("ksq", c.c("ksq"))] +
                                  ([("ch", c.c("ch")), ("fin", c.c("fin"))]
                                   if jj == 0 else []),
                            incs=[("qn", 1)])
                    c.op("vector", lambda e: e.tensor_copy(rowslot2[:],
                                                           psS[0:1, 0:512]),
                         waits=[("qn", c.c("qn"))], incs=[("ch", 1)])
                    for q4 in range(4):
                        c.op("tensor", lambda e, q4=q4: e.transpose(
                            psS[:, 256:257], rowslot2[0:1, q4 * P:(q4 + 1) * P],
                            id_f32[0:1, 0:1]),
                            waits=[("ch", c.c("ch")), ("fin", c.c("fin"))],
                            incs=[("aux", 1)])
                        col = 64 + tc * 4 + q4
                        if jh == 0:
                            c.op("vector", lambda e, col=col: e.tensor_copy(
                                norms[:, col:col + 1], psS[:, 256:257]),
                                waits=[("aux", c.c("aux"))], incs=[("fin", 1)])
                        else:
                            c.op("vector", lambda e, col=col: e.tensor_add(
                                norms[:, col:col + 1], norms[:, col:col + 1],
                                psS[:, 256:257]),
                                waits=[("aux", c.c("aux"))], incs=[("fin", 1)])
            q_done = c.c("kv")
            if C_SUB == "q":
                return
            c.op("scalar", lambda e: e.activation(out=norms[:, 80:96],
                                                  in_=norms[:, 64:80], func=AF.Sqrt,
                                                  bias=consts[:, 2:3]),
                 waits=[("fin", c.c("fin"))], incs=[("chs", 1)])
            c.op("vector", lambda e: e.tensor_scalar_add(norms[:, 80:96],
                                                         norms[:, 80:96], 1e-8),
                 waits=[("chs", c.c("chs"))], incs=[("ch", 1)])
            c.op("vector", lambda e: e.reciprocal(norms[:, 80:96], norms[:, 80:96]),
                 waits=[("ch", c.c("ch"))], incs=[("ch", 1)])
            colmax_scalar(c, norms[:, 64:80], P, smalls[0:1, 9:10],
                          [("ch", c.c("ch"))])
            c.op("vector", lambda e: e.tensor_mul(smalls[0:1, 10:11],
                                                  smalls[0:1, 8:9],
                                                  smalls[0:1, 9:10]),
                 waits=[("fin", c.c("fin"))], incs=[("fin", 1)])
            c.op("scalar", lambda e: e.activation(out=smalls[0:1, 11:12],
                                                  in_=smalls[0:1, 10:11],
                                                  func=AF.Sqrt,
                                                  bias=consts[0:1, 2:3]),
                 waits=[("fin", c.c("fin"))], incs=[("chs", 1)])
            c.op("vector", lambda e: e.tensor_scalar_mul(smalls[0:1, 12:13],
                                                         smalls[0:1, 11:12],
                                                         -1.0 / 32.0),
                 waits=[("chs", c.c("chs"))], incs=[("fin", 1)])
            bcast_col(c, smalls[0:1, 12:13], smalls[:, 13:14],
                      [("fin", c.c("fin"))])
            for tq in range(NTH):
                c.op("vector", lambda e, tq=tq: e.tensor_scalar(
                    sqs[:, 0:P], id_f32[:], norms[:, 80 + tq:81 + tq], None,
                    op0=ALU.mult),
                    waits=[("ch", c.c("ch")), ("lns", c.c("lns")),
                           ("aux", c.c("aux"))],
                    incs=[("ch", 1)])
                c.op("tensor", lambda e, tq=tq: e.matmul(
                    psS[:, 0:128], allones[:], sqs[:, 0:P],
                    start=True, stop=True),
                    waits=[("ch", c.c("ch")), ("aux2", c.c("aux2"))],
                    incs=[("aux", 1)])
                c.op("scalar", lambda e, tq=tq: e.copy(
                    rqb[:, tq * P:(tq + 1) * P], psS[:, 0:128]),
                    waits=[("aux", c.c("aux"))], incs=[("aux2", 1)])
            rqb_done = c.c("aux2")
            for bh in range(2):
                c.op("sync", lambda e, bh=bh: e.dma_start(
                    out=rowslot[:], in_=brow_h[3:4, bh * 512:(bh + 1) * 512]),
                    waits=[("aux", c.c("aux")), ("acc", c.c("acc"))],
                    incs=[("ld", 16)])
                c.op("tensor", lambda e: e.matmul(
                    psS[:, 0:512], allones[0:1, :], rowslot[:],
                    start=True, stop=True),
                    waits=[("ld", c.c("ld")), ("aux2", c.c("aux2"))],
                    incs=[("aux", 1)])
                c.op("scalar", lambda e, bh=bh: e.copy(
                    C_bob[:, bh * 512:(bh + 1) * 512], psS[:, 0:512]),
                    waits=[("aux", c.c("aux")), ("rt", c.c("rt"))],
                    incs=[("aux2", 1)])
            bob_rdy = c.c("aux2")
            if C_SUB == "pre":
                return
            for qc in range(8):
                q_sl = slice(qc * 256, (qc + 1) * 256)
                for kc in range(NT):
                    sc_ps = psE if kc % 2 == 0 else psF
                    first = (qc == 0 and kc == 0)
                    for ci in range(NCI):
                        qkw = []
                        if ci == 0:
                            qkw = [("ch", max(0, c.c("ch") - 3))]
                            if kc <= 1:
                                qkw += [("res", c.c("res"))]
                            if first:
                                qkw += [("kv", q_done), ("aux2", bob_rdy),
                                        ("ksq", c.c("ksq"))]
                        c.op("tensor", lambda e, kc=kc, ci=ci, sc_ps=sc_ps, q_sl=q_sl: e.matmul(
                            sc_ps[:, 0:256], K_fm[:, ci, kc * P:(kc + 1) * P],
                            C_Q[:, ci, q_sl],
                            start=(ci == 0), stop=(ci == NCI - 1)),
                            waits=qkw,
                            incs=([("sc", 1)] if ci == NCI - 1 else []))
                    c.op("vector", lambda e, sc_ps=sc_ps, q_sl=q_sl: e.tensor_tensor(
                        sqs[:, 0:256], sc_ps[:, 0:256], rqb[:, q_sl], ALU.mult),
                        waits=[("sc", c.c("sc")), ("chs", c.c("chs")),
                               ("ex", c.c("ex"))],
                        incs=[("ch", 1)])
                    c.op("scalar", lambda e, kc=kc: e.activation(
                        out=sqs[:, 256:512], in_=sqs[:, 0:256], func=AF.Abs,
                        scale=norms[:, 32 + kc:33 + kc]),
                        waits=[("ch", c.c("ch"))], incs=[("chs", 1)])
                    c.op("vector", lambda e: e.tensor_scalar_min(
                        sqs[:, 256:512], sqs[:, 256:512], PI),
                        waits=[("chs", c.c("chs"))], incs=[("ch", 1)])
                    c.op("scalar", lambda e: e.activation(
                        out=sqs[:, 256:512], in_=sqs[:, 256:512], func=AF.Sin,
                        scale=-1.0, bias=consts[:, 1:2]),
                        waits=[("ch", c.c("ch"))], incs=[("chs", 1)])
                    c.op("vector", lambda e, sc_ps=sc_ps: e.scalar_tensor_tensor(
                        out=sqs[:, 0:256], in0=sc_ps[:, 0:256], scalar=1.0 / 32.0,
                        in1=sqs[:, 256:512], op0=ALU.mult, op1=ALU.mult),
                        waits=[("chs", c.c("chs"))], incs=[("ch", 1)])
                    pt = C_pt[:, (kc % 2) * 256:(kc % 2 + 1) * 256]
                    c.op("scalar", lambda e, pt=pt: e.activation(
                        out=pt, in_=sqs[:, 0:256], func=AF.Exp,
                        bias=smalls[:, 13:14]),
                        waits=[("ch", c.c("ch")),
                               ("pv", max(0, c.c("pv") - 1))],
                        incs=[("ex", 1)])
                    exn = c.c("ex")
                    for sub in range(2):
                        for jjh in range(2):
                            bank = [psAB[:, 0:512], psAB[:, 512:1024],
                                    psCD[:, 0:512], psCD[:, 512:1024]][sub * 2 + jjh]
                            pvw = []
                            if sub == 0 and jjh == 0:
                                pvw = [("ex", exn)]
                                if kc == 0:
                                    pvw += [("fin", c.c("fin")),
                                            ("chs", c.c("chs")),
                                            ("aux", c.c("aux"))]
                            c.op("tensor", lambda e, kc=kc, sub=sub, jjh=jjh, bank=bank, pt=pt: e.matmul(
                                bank, pt[:, sub * P:(sub + 1) * P],
                                V_tm[:, kc, jjh * 512:(jjh + 1) * 512],
                                start=(kc == 0), stop=(kc == NT - 1)),
                                waits=pvw, incs=[])
                    c.op("tensor", lambda e, kc=kc, pt=pt: e.matmul(
                        psS[0:1, 0:256], onesr_b[:], pt[:],
                        start=(kc == 0), stop=(kc == NT - 1)),
                        waits=([("fin", c.c("fin"))] if kc == 0 else []),
                        incs=[("pv", 1)])
                c.op("vector", lambda e: e.reciprocal(rowslot[0:1, 0:256],
                                                      psS[0:1, 0:256]),
                     waits=[("pv", c.c("pv"))],
                     incs=[("fin", 1)])
                for sub in range(2):
                    c.op("tensor", lambda e, sub=sub: e.transpose(
                        psS[:, 300 + sub:301 + sub],
                        rowslot[0:1, sub * P:(sub + 1) * P], id_f32[0:1, 0:1]),
                        waits=[("fin", c.c("fin"))], incs=[("aux", 1)])
                c.op("vector", lambda e: e.tensor_copy(norms[:, 90:92],
                                                       psS[:, 300:302]),
                     waits=[("aux", c.c("aux"))], incs=[("fin", 1)])
                for sub in range(2):
                    for jjh in range(2):
                        bank = [psAB[:, 0:512], psAB[:, 512:1024],
                                psCD[:, 0:512], psCD[:, 512:1024]][sub * 2 + jjh]
                        c.op("scalar", lambda e, sub=sub, jjh=jjh, bank=bank: e.activation(
                            out=h_tm[:, jjh * 512:(jjh + 1) * 512], in_=bank,
                            func=AF.Copy, scale=norms[:, 90 + sub:91 + sub]),
                            waits=[("fin", c.c("fin"))] +
                                  ([("tr", c.c("tr"))] if jjh == 0 else []),
                            incs=[("chs", 1)])
                    for k in range(NCI):
                        c.op("tensor", lambda e, k=k: e.transpose(
                            psT[:, k * P:(k + 1) * P], h_tm[:, k * P:(k + 1) * P],
                            id_bf[:]),
                            waits=([("chs", c.c("chs")), ("hfm", c.c("hfm"))]
                                   if k == 0 else []),
                            incs=([("tr", 1)] if k == NCI - 1 else []))
                    c.op("vector", lambda e, sub=sub: e.tensor_copy(
                        C_aoqc[:, :, sub * P:(sub + 1) * P], psT[:, 0:C]),
                        waits=[("tr", c.c("tr")), ("yev", c.c("yev")),
                               ("ms", c.c("ms"))],
                        incs=[("hfm", 1)])
                aoq_rdy = c.c("hfm")
                for jjh in range(2):
                    c.op("gpsimd", lambda e, jjh=jjh: e.dma_start(
                        out=C_wsl[:, :, :],
                        in_=woT.rearrange("(ci p) j -> p ci j", p=P)
                        [:, :, jjh * 512:(jjh + 1) * 512]),
                        waits=[("yev", c.c("yev")), ("sc", c.c("sc"))]
                              if (qc == 0 and jjh == 0) else
                              [("yev", c.c("yev"))],
                        incs=[("wa", 16)])
                    wo_rdy = c.c("wa")
                    for sub in range(2):
                        gq = qc * 2 + sub
                        c.op("sync", lambda e, gq=gq, sub=sub, jjh=jjh: e.dma_start(
                            out=x_tile[:, sub * 512:(sub + 1) * 512],
                            in_=xQ[gq * P:(gq + 1) * P,
                                   jjh * 512:(jjh + 1) * 512]),
                            waits=[("res", c.c("res")), ("ex", c.c("ex")),
                                   ("ms", c.c("ms"))],
                            incs=[("xin", 16)])
                        xw = c.c("xin")
                        pse = psE if sub == 0 else psF
                        for ci in range(NCI):
                            c.op("tensor", lambda e, sub=sub, jjh=jjh, ci=ci, pse=pse: e.matmul(
                                pse[:], C_aoqc[:, ci, sub * P:(sub + 1) * P],
                                C_wsl[:, ci, :],
                                start=(ci == 0), stop=(ci == NCI - 1)),
                                waits=([("hfm", aoq_rdy), ("wa", wo_rdy),
                                        ("res", max(0, c.c("res") - 2))]
                                       if ci == 0 else []),
                                incs=([("yev", 1)] if ci == NCI - 1 else []))
                        c.op("vector", lambda e, jjh=jjh, pse=pse: e.tensor_add(
                            rlbuf[:], pse[:], C_bob[:, jjh * 512:(jjh + 1) * 512]),
                            waits=[("yev", c.c("yev")), ("ksq", c.c("ksq")),
                                   ("xst", c.c("xst"))],
                            incs=[("res", 1)])
                        c.op("vector", lambda e, sub=sub: e.tensor_add(
                            rlbuf[:], rlbuf[:],
                            x_tile[:, sub * 512:(sub + 1) * 512]),
                            waits=[("xin", xw)],
                            incs=[("res", 1)])
                        c.op("sync", lambda e, gq=gq, jjh=jjh: e.dma_start(
                            out=xC[gq * P:(gq + 1) * P, jjh * 512:(jjh + 1) * 512],
                            in_=rlbuf[:]),
                            waits=[("res", c.c("res"))], incs=[("xst", 16)])

        def emit_tap(c):
            last = STAGES[-1]
            if last == "D":
                c.op("sync", lambda e: (e.wait_ge(c.sems["out"], c.c("out")), None)[1])
                return
            eff = last
            if last == "C" and C_SUB != "full":
                eff = "B"
            src = {"A": xA, "B": xB, "C": xC}[eff]
            for tt in range(NTH):
                if eff in ("A", "B"):
                    inap = roff(c, src[tt * P:(tt + 1) * P, :])
                else:
                    inap = src[tt * P:(tt + 1) * P, :]
                c.op("sync", lambda e, tt=tt, inap=inap: e.dma_start(
                    out=x_tile[:], in_=inap),
                    waits=[("xst", c.c("xst")), ("out", c.c("out"))],
                    incs=[("xin2", 16)])
                c.op("sync", lambda e, tt=tt: e.dma_start(
                    out=y_out[tt * P:(tt + 1) * P, :], in_=x_tile[:]),
                    waits=[("xin2", c.c("xin2"))], incs=[("out", 16)])
            c.op("sync", lambda e: (e.wait_ge(c.sems["out"], c.c("out")), None)[1])

        def program(c):
            emit_consts(c)
            phase_h1(c)
            if "A" in STAGES:
                stage_A(c)
            c.barrier()
            if "B" in STAGES:
                stage_ffn(c, "B")
                c.barrier()
            if "C" in STAGES:
                stage_C(c)
                c.barrier()
            if "D" in STAGES:
                stage_ffn(c, "D")
                c.barrier()
            emit_tap(c)

        with nc.Block() as blk:
            for engname in ENGINES:
                def make_emit(en):
                    def emit(eng):
                        ctx.start(en, eng)
                        program(ctx)
                    return emit
                getattr(blk, engname)(make_emit(engname))
    return nc


_NC_CACHE = None


def _get_nc():
    global _NC_CACHE
    if _NC_CACHE is None:
        _NC_CACHE = build_nc()
    return _NC_CACHE


def kernel(**inputs):
    f = lambda k: np.ascontiguousarray(np.asarray(inputs[k], dtype=np.float32))
    x = f("L0_attn_a_x")
    W_attn = f("L0_attn_a_W_attn")
    W_proj = f("L0_attn_a_W_proj")
    Wq, bq = f("L0_attn_b_Wq"), f("L0_attn_b_bq")
    Wk, bk = f("L0_attn_b_Wk"), f("L0_attn_b_bk")
    Wv, bv = f("L0_attn_b_Wv"), f("L0_attn_b_bv")
    Wo, bo = f("L0_attn_b_Wo"), f("L0_attn_b_bo")
    gate_w = f("L0_ffn_gate_w")
    gate_up_w = f("L0_ffn_gate_up_w")
    down_w = f("L0_ffn_down_w")

    biases = np.ascontiguousarray(np.stack([bq, bk, bv, bo]))
    bias_hc = np.ascontiguousarray(biases.reshape(4, NCI, P).transpose(2, 0, 1))
    gw_hc = np.ascontiguousarray(gate_w.T.reshape(NCI, P, NE).transpose(1, 0, 2))
    wqT = np.ascontiguousarray(Wq.T)
    wkT = np.ascontiguousarray(Wk.T)
    wvT = np.ascontiguousarray(Wv.T)
    woT = np.ascontiguousarray(Wo.T)
    guwT = np.ascontiguousarray(gate_up_w.transpose(0, 2, 1))
    dwT = np.ascontiguousarray(down_w.transpose(0, 2, 1))

    nc = _get_nc()
    in_maps = []
    for core in range(8):
        b = core // 2
        in_maps.append({
            "x0": np.ascontiguousarray(x[b]),
            "wattn": W_attn, "wproj": W_proj,
            "wqT": wqT, "wkT": wkT, "wvT": wvT, "woT": woT,
            "bias_h": bias_hc, "brow_h": biases, "gw_h": gw_hc,
            "guwT": guwT, "dwT": dwT,
        })
    res = run_bass_kernel_spmd(nc, in_maps, core_ids=list(range(8)))
    out = np.empty((4, T, C), dtype=np.float32)
    for core in range(8):
        b, h = core // 2, core % 2
        out[b, h * TH:(h + 1) * TH, :] = res.results[core]["y"]
    return out


# version 04
# speedup vs baseline: 1.0481x; 1.0481x over previous
import math
import dataclasses
import numpy as np
from contextlib import ExitStack

import concourse.bass as bass
import concourse.mybir as mybir
from concourse.bass_utils import run_bass_kernel_spmd

F32 = mybir.dt.float32
BF16 = mybir.dt.bfloat16
AF = mybir.ActivationFunctionType
ALU = mybir.AluOpType
AX = mybir.AxisListType.X

P = 128
T = 4096
TH = 2048
C = 1024
NT = T // P
NTH = TH // P
NCI = C // P
NE = 32
NPAIR = 8
EPS = 1e-5
PI = math.pi

STAGES = "ABCD"
DEBUG_DUMPS = False
C_SUB = "full"

ENGINES = ["sync", "tensor", "vector", "scalar", "gpsimd"]


class Ctx:
    def __init__(self, nc):
        self.nc = nc
        self.sems = {}
        self.counts = {}
        self.cur = None
        self.eng = None
        self.es = None
        self._q0 = None

    def make_sem(self, name):
        h = self.es.enter_context(self.nc.semaphore(name))
        self.sems[name] = h
        self.counts[name] = 0

    def start(self, engname, eng):
        self.cur = engname
        self.eng = eng
        self._q0 = None
        for k in self.counts:
            self.counts[k] = 0

    def c(self, name):
        return self.counts[name]

    def q0row(self):
        if self._q0 is None:
            cid = self.eng.partition_id()
            self._q0 = (cid % 2) * (TH * C)
        return self._q0

    def op(self, engname, fn, waits=(), incs=()):
        if self.cur == engname:
            e = self.eng
            for s, v in waits:
                if v > 0:
                    e.wait_ge(self.sems[s], v)
            inst = fn(e)
            if incs:
                assert inst is not None, "op with incs must return an instruction"
                for s, n in incs:
                    inst.then_inc(self.sems[s], n)
        for s, n in incs:
            self.counts[s] += n

    def barrier(self):
        for en in ENGINES:
            self.op(en, lambda e: e.nop(), incs=[("bar", 1)])
        tgt = self.c("bar")
        for en in ENGINES:
            self.op(en, lambda e, t=tgt: (e.wait_ge(self.sems["bar"], t), None)[1])


def roff(ctx, ap):
    return dataclasses.replace(ap, offset=ap.offset + ctx.q0row())


SEMS = [
    "bar", "cst", "ld", "xin", "xin2", "lnv", "lns", "tr", "hfm",
    "wa", "prj", "prjv", "sc", "ms", "ex", "pv", "fin", "aux", "aux2",
    "aof", "yev", "res", "xst",
    "wl", "wf", "rt", "gu", "guv", "gus", "dn", "acc",
    "kv", "ksq", "qn", "ch", "chs", "l0", "out",
]


def build_nc():
    nc = bass.Bass("TRN2", target_bir_lowering=False, debug=False,
                   detect_race_conditions=False)
    ctx = Ctx(nc)
    es = ExitStack()
    ctx.es = es
    with es:
        for s in SEMS:
            ctx.make_sem(s)

        x0 = nc.dram_tensor("x0", [T, C], F32, kind="ExternalInput").ap()
        wattn = nc.dram_tensor("wattn", [C, 3 * C], F32, kind="ExternalInput").ap()
        wproj = nc.dram_tensor("wproj", [C, C], F32, kind="ExternalInput").ap()
        wqT = nc.dram_tensor("wqT", [C, C], F32, kind="ExternalInput").ap()
        wkT = nc.dram_tensor("wkT", [C, C], F32, kind="ExternalInput").ap()
        wvT = nc.dram_tensor("wvT", [C, C], F32, kind="ExternalInput").ap()
        woT = nc.dram_tensor("woT", [C, C], F32, kind="ExternalInput").ap()
        bias_h = nc.dram_tensor("bias_h", [P, 4, NCI], F32, kind="ExternalInput").ap()
        brow_h = nc.dram_tensor("brow_h", [4, C], F32, kind="ExternalInput").ap()
        gw_h = nc.dram_tensor("gw_h", [P, NCI, NE], F32, kind="ExternalInput").ap()
        guwT = nc.dram_tensor("guwT", [NE, C, C], F32, kind="ExternalInput").ap()
        dwT = nc.dram_tensor("dwT", [NE, 512, C], F32, kind="ExternalInput").ap()
        y_out = nc.dram_tensor("y", [TH, C], F32, kind="ExternalOutput").ap()
        xA = nc.dram_tensor("xA", [T, C], F32).ap()
        xB = nc.dram_tensor("xB", [T, C], F32).ap()
        xC = nc.dram_tensor("xC", [TH, C], F32).ap()
        xQ = nc.dram_tensor("xQ", [TH, C], F32).ap()
        if DEBUG_DUMPS:
            dbg_q = nc.dram_tensor("dbg_q", [P, 4096], BF16, kind="ExternalOutput").ap()
            dbg_k = nc.dram_tensor("dbg_k", [P, 4096], BF16, kind="ExternalOutput").ap()
            dbg_vi = nc.dram_tensor("dbg_vi", [P, 4160], BF16, kind="ExternalOutput").ap()
            dbg_sm = nc.dram_tensor("dbg_sm", [P, 16], F32, kind="ExternalOutput").ap()
            dbg_h = nc.dram_tensor("dbg_h", [P, 4096], BF16, kind="ExternalOutput").ap()
            dbg_ao = nc.dram_tensor("dbg_ao", [P, 4096], BF16, kind="ExternalOutput").ap()

        sb = lambda n, s, d=F32: es.enter_context(nc.sbuf_tensor(n, s, d))
        pool_x = sb("pool_x", [P, NCI * T], BF16)
        pool_y = sb("pool_y", [P, NTH * C], F32)
        pool_z = sb("pool_z", [P, 22528], BF16)
        msks = sb("msks", [P, 4096], BF16)
        x_tile = sb("x_tile", [P, C], F32)
        h_tm = sb("h_tm", [P, C], BF16)
        sqs = sb("sqs", [P, C], F32)
        rqb = sb("rqb", [P, TH], F32)
        rowslot = sb("rowslot", [1, 512], F32)
        rowslot2 = sb("rowslot2", [1, 512], F32)
        rlbuf = sb("rlbuf", [P, 512], F32)
        stat = sb("stat", [P, 2, 8], F32)
        consts = sb("consts", [P, 8], F32)
        bias_sb = sb("bias_sb", [P, 4, NCI], F32)
        gw_sb = sb("gw_sb", [P, NCI, NE], BF16)
        id_bf = sb("id_bf", [P, P], BF16)
        id_f32 = sb("id_f32", [P, P], F32)
        allones = sb("allones", [P, P], F32)
        ones1f = sb("ones1f", [P, 1], F32)
        ones2 = sb("ones2", [P, 2], BF16)
        onesr_b = sb("onesr_b", [P, 1], BF16)
        cmask = sb("cmask", [P, 2, P], F32)
        norms = sb("norms", [P, 96], F32)
        smalls = sb("smalls", [P, 16], F32)
        probs = sb("probs", [P, NE], F32)
        m8 = sb("m8", [P, 8], F32)
        pm = sb("pm", [P, NE], F32)

        ps = lambda n, s, d=F32: es.enter_context(nc.psum_tensor(n, s, d))
        psAB = ps("psAB", [P, 1024])
        psCD = ps("psCD", [P, 1024])
        psE = ps("psE", [P, 512])
        psF = ps("psF", [P, 512])
        psT = ps("psT", [P, 1024], BF16)
        psS = ps("psS", [P, 512])

        h_fm = pool_x[:].rearrange("p (ci t) -> p ci t", ci=NCI)
        K_fm = h_fm
        h4_fm = pool_x[:, 0:NCI * TH].rearrange("p (ci t) -> p ci t", ci=NCI)
        ao_fm = pool_y[:].bitcast(BF16).rearrange("p (ci t) -> p ci t", ci=NCI)
        acc_v = pool_y[:].rearrange("p (tt c) -> p tt c", tt=NTH)
        V_tm = pool_y[:].bitcast(BF16).rearrange("p (tt c) -> p tt c", tt=NT)

        zb = pool_z
        A_Qp = zb[:, 0:4096]
        A_Kp = zb[:, 4096:8192]
        A_VI = zb[:, 8192:8192 + 32 * 130].rearrange("p (tt v) -> p tt v", tt=NT)
        wpair = zb[:, 12352:15424].rearrange("p (ci w k) -> p ci w k", ci=NCI, w=3)
        A_sq = zb[:, 15424:15936]
        A_wproj = zb[:, 0:8192].rearrange("p (ci j) -> p ci j", ci=NCI)
        B_guw = zb[:, 0:16384].rearrange("p (b ci j) -> p b ci j", b=2, ci=NCI)
        B_dw = zb[:, 16384:20480].rearrange("p (ci j) -> p ci j", ci=4)
        B_sg = zb[:, 20480:22528].rearrange("p (g t) -> p g t", g=4)
        C_Q = zb[:, 0:16384].rearrange("p (ci t) -> p ci t", ci=NCI)
        C_wk = zb[:, 0:8192].rearrange("p (ci j) -> p ci j", ci=NCI)
        C_wv = zb[:, 8192:16384].rearrange("p (ci j) -> p ci j", ci=NCI)
        C_wsl = zb[:, 16384:20480].rearrange("p (ci j) -> p ci j", ci=NCI)
        C_wq2 = msks[:, 0:4096].rearrange("p (ci j) -> p ci j", ci=NCI)
        C_bvb = zb[:, 20480:22528].bitcast(F32)

        mask01 = msks[:, 0:2048].rearrange("p (m k) -> p m k", m=4)
        B_t = msks[:, 0:2048].rearrange("p (g t) -> p g t", g=4)
        comb = msks[:, 2048:4096].bitcast(F32).rearrange(
            "p (tt e) -> p tt e", tt=NT)
        C_aoqc = msks[:, 0:2048].rearrange("p (ci q) -> p ci q", ci=NCI)
        C_bob = msks[:, 2048:4096].bitcast(F32)

        A_pt = x_tile[:].bitcast(BF16)
        C_pt = sqs[:].bitcast(BF16)[:, 1024:1536]
        hq_v = rqb[:].bitcast(BF16).rearrange("p (ci t) -> p ci t", ci=NCI)
        lrow = rqb

        def emit_consts(c):
            def base(e):
                e.memset(consts[:, 0:1], EPS)
                e.memset(consts[:, 1:2], PI / 2)
                e.memset(consts[:, 2:3], 0.0)
                e.memset(allones[:], 1.0)
                e.memset(ones1f[:], 1.0)
                e.memset(onesr_b[:], 1.0)
                e.memset(ones2[0:64, 0:1], 1.0)
                e.memset(ones2[64:128, 0:1], 0.0)
                e.memset(ones2[0:64, 1:2], 0.0)
                e.memset(ones2[64:128, 1:2], 1.0)
                e.memset(cmask[0:64, :, :], 0.0)
                e.memset(cmask[64:128, 0, 0:64], 1.0)
                e.memset(cmask[64:128, 0, 64:128], 0.0)
                e.memset(cmask[64:128, 1, 0:64], 0.0)
                e.memset(cmask[64:128, 1, 64:128], 1.0)
                e.memset(A_VI[:, :, 64:65], 1.0)
                e.memset(id_bf[:], 0.0)
                e.memset(id_f32[:], 0.0)
                return e.memset(A_VI[:, :, 129:130], 1.0)
            c.op("gpsimd", base, incs=[("cst", 1)])
            c.op("gpsimd", lambda e: e.affine_select(
                out=id_bf[:], in_=id_bf[:], compare_op=ALU.not_equal,
                fill=1.0, base=0, pattern=[[-1, P]], channel_multiplier=1),
                waits=[("cst", c.c("cst"))], incs=[("cst", 1)])
            c.op("gpsimd", lambda e: e.affine_select(
                out=id_f32[:], in_=id_f32[:], compare_op=ALU.not_equal,
                fill=1.0, base=0, pattern=[[-1, P]], channel_multiplier=1),
                waits=[("cst", c.c("cst"))], incs=[("cst", 1)])
            for m in range(4):
                c.op("gpsimd", lambda e: e.memset(sqs[:, 0:512], 1.0),
                     waits=[("ms", c.c("ms"))], incs=[("cst", 1)])
                c.op("gpsimd", lambda e, m=m: e.affine_select(
                    out=sqs[:, 0:512], in_=sqs[:, 0:512], compare_op=ALU.is_ge,
                    fill=0.0, base=-128 * m, pattern=[[1, 512]],
                    channel_multiplier=-1),
                    waits=[("cst", c.c("cst"))], incs=[("cst", 1)])
                c.op("vector", lambda e, m=m: e.tensor_copy(mask01[:, m], sqs[:, 0:512]),
                     waits=[("cst", c.c("cst"))], incs=[("ms", 1)])
            c.op("sync", lambda e: e.dma_start(out=bias_sb[:], in_=bias_h),
                 incs=[("ld", 16)])
            c.op("gpsimd", lambda e: e.dma_start(out=gw_sb[:], in_=gw_h),
                 waits=[("ms", c.c("ms"))], incs=[("wl", 16)])

        def emit_ln(c, x_ap, par, xwaits, dest_fm_ap=None, pst=None):
            st = lambda i: stat[:, par, i:i + 1]
            c.op("vector", lambda e: e.reduce_sum(st(0), x_ap, axis=AX),
                 waits=list(xwaits) + [("lns", c.c("lns"))], incs=[("lnv", 1)])
            c.op("scalar", lambda e: e.activation(out=sqs[:], in_=x_ap,
                                                  func=AF.Square),
                 waits=list(xwaits) + [("lnv", c.c("lnv"))], incs=[("lns", 1)])
            c.op("vector", lambda e: e.tensor_scalar_mul(st(2), st(0), 1.0 / C),
                 waits=[("lnv", c.c("lnv"))], incs=[("lnv", 1)])
            c.op("vector", lambda e: e.reduce_sum(st(1), sqs[:], axis=AX),
                 waits=[("lns", c.c("lns"))], incs=[("lnv", 1)])
            c.op("vector", lambda e: e.tensor_scalar_mul(st(3), st(1), 1.0 / C),
                 waits=[("lnv", c.c("lnv"))], incs=[("lnv", 1)])
            c.op("vector", lambda e: e.tensor_mul(st(4), st(2), st(2)),
                 waits=[("lnv", c.c("lnv")), ("lns", c.c("lns"))], incs=[("lnv", 1)])
            c.op("vector", lambda e: e.tensor_sub(st(4), st(3), st(4)),
                 waits=[("lnv", c.c("lnv"))], incs=[("lnv", 1)])
            c.op("scalar", lambda e: e.activation(out=st(5), in_=st(4), func=AF.Sqrt,
                                                  bias=consts[:, 0:1]),
                 waits=[("lnv", c.c("lnv"))], incs=[("lns", 1)])
            c.op("vector", lambda e: e.reciprocal(st(6), st(5)),
                 waits=[("lns", c.c("lns"))], incs=[("lnv", 1)])
            c.op("vector", lambda e: e.tensor_scalar(
                out=h_tm[:], in0=x_ap, scalar1=st(2), scalar2=st(6),
                op0=ALU.subtract, op1=ALU.mult),
                waits=[("lnv", c.c("lnv"))], incs=[("lnv", 1)])
            if dest_fm_ap is not None:
                hs = c.c("lnv")
                p2 = pst if pst is not None else psT[:]
                for k in range(NCI):
                    c.op("tensor", lambda e, k=k, p2=p2: e.transpose(
                        p2[:, k * P:(k + 1) * P], h_tm[:, k * P:(k + 1) * P],
                        id_bf[:]),
                        waits=([("lnv", hs), ("hfm", c.c("hfm"))] if k == 0 else []),
                        incs=([("tr", 1)] if k == NCI - 1 else []))
                c.op("vector", lambda e, p2=p2: e.tensor_copy(dest_fm_ap, p2[:, 0:C]),
                     waits=[("tr", c.c("tr"))], incs=[("hfm", 1)])

        def phase_h1(c):
            lnw = {}
            for qh in range(4):
                sl = (qh % 2) * 8
                w = ([("ms", c.c("ms"))] if qh == 0 else
                     ([] if qh == 1 else lnw[qh - 2]))
                c.op("sync", lambda e, qh=qh, sl=sl: e.dma_start(
                    out=acc_v[:, sl:sl + 8, :],
                    in_=x0[qh * 1024:(qh + 1) * 1024, :].rearrange(
                        "(tt p) c -> p tt c", p=P)),
                    waits=w, incs=[("xin2", 16)])
                for tt in range(8):
                    gt = qh * 8 + tt
                    emit_ln(c, acc_v[:, sl + tt, :], gt % 2,
                            [("xin2", c.c("xin2"))],
                            dest_fm_ap=h_fm[:, :, gt * P:(gt + 1) * P],
                            pst=(psCD[:].bitcast(BF16) if gt % 2 else None))
                lnw[qh] = [("lnv", c.c("lnv")), ("lns", c.c("lns"))]

        def colmax_scalar(c, col_ap, k, dst_ap, wv):
            c.op("vector", lambda e: e.reduce_max(norms[0:k, 94:95], col_ap,
                                                  axis=AX),
                 waits=list(wv) + [("aux", c.c("aux"))], incs=[("fin", 1)])
            c.op("tensor", lambda e: e.transpose(psS[0:1, 256:256 + k],
                                                 norms[0:k, 94:95],
                                                 id_f32[0:k, 0:k]),
                 waits=[("fin", c.c("fin"))], incs=[("aux", 1)])
            c.op("vector", lambda e: e.reduce_max(dst_ap, psS[0:1, 256:256 + k],
                                                  axis=AX),
                 waits=[("aux", c.c("aux"))], incs=[("fin", 1)])

        def bcast_col(c, src11_ap, dst_col_ap, wv):
            c.op("tensor", lambda e: e.matmul(psS[:, 264:265], allones[0:1, :],
                                              src11_ap, start=True, stop=True),
                 waits=list(wv) + [("aux2", c.c("aux2"))], incs=[("aux", 1)])
            c.op("scalar", lambda e: e.copy(dst_col_ap, psS[:, 264:265]),
                 waits=[("aux", c.c("aux"))], incs=[("aux2", 1)])

        def stage_A(c):
            for pr in range(NPAIR):
                wsrc = wattn.rearrange("(ci p) n -> p ci n", p=P)
                for j, base in enumerate((0, C, 2 * C)):
                    c.op("gpsimd", lambda e, j=j, base=base, pr=pr: e.dma_start(
                        out=wpair[:, :, j, :],
                        in_=wsrc[:, :, base + pr * P: base + (pr + 1) * P]),
                        waits=([("pv", c.c("pv")), ("aux", c.c("aux"))]
                               if (pr > 0 and j == 0) else []),
                        incs=[("wa", 16)])
                wl = c.c("wa")
                hfm_all = c.c("hfm")
                for which, dst in ((0, A_Qp), (1, A_Kp)):
                    for tc in range(8):
                        pse = psE if (tc + which * 8) % 2 == 0 else psF
                        for ci in range(NCI):
                            c.op("tensor", lambda e, ci=ci, tc=tc, which=which, pse=pse: e.matmul(
                                pse[:], wpair[:, ci, which, :],
                                h_fm[:, ci, tc * 512:(tc + 1) * 512],
                                start=(ci == 0), stop=(ci == NCI - 1)),
                                waits=([("wa", wl), ("hfm", hfm_all),
                                        ("prj", max(0, c.c("prj") - 2)),
                                        ("prjv", max(0, c.c("prjv") - 1))]
                                       if ci == 0 else []),
                                incs=([("sc", 1)] if ci == NCI - 1 else []))
                        c.op("scalar", lambda e, tc=tc, dst=dst, pse=pse: e.activation(
                            out=dst[:, tc * 512:(tc + 1) * 512], in_=pse[:],
                            func=AF.Copy), waits=[("sc", c.c("sc"))],
                            incs=[("prj", 1)])
                        c.op("scalar", lambda e, pse=pse: e.activation(
                            out=A_sq[:, :], in_=pse[:], func=AF.Square),
                            waits=[("aux", c.c("aux"))], incs=[("prj", 1)])
                        c.op("tensor", lambda e: e.matmul(
                            psS[0:2, 0:512], ones2[:], A_sq[:, :],
                            start=True, stop=True),
                            waits=[("prj", c.c("prj")), ("prjv", c.c("prjv"))],
                            incs=[("aux", 1)])
                        slot = which * 8 + tc
                        c.op("vector", lambda e, slot=slot: e.reduce_max(
                            norms[0:2, 64 + slot:65 + slot], psS[0:2, 0:512], axis=AX),
                            waits=[("aux", c.c("aux"))], incs=[("prjv", 1)])
                c.op("vector", lambda e: e.reduce_max(norms[0:2, 80:81],
                                                      norms[0:2, 64:80], axis=AX),
                     waits=[("prjv", c.c("prjv"))], incs=[("fin", 1)])
                c.op("tensor", lambda e: e.transpose(psS[0:1, 256:258],
                                                     norms[0:2, 80:81],
                                                     id_f32[0:2, 0:2]),
                     waits=[("fin", c.c("fin"))], incs=[("aux", 1)])
                c.op("vector", lambda e: e.reduce_max(smalls[0:1, 0:1],
                                                      psS[0:1, 256:258], axis=AX),
                     waits=[("aux", c.c("aux"))], incs=[("fin", 1)])
                c.op("vector", lambda e: e.tensor_scalar_mul(smalls[0:1, 3:4],
                                                             smalls[0:1, 0:1],
                                                             -0.125),
                     waits=[("fin", c.c("fin"))], incs=[("fin", 1)])
                bcast_col(c, smalls[0:1, 3:4], smalls[:, 4:5],
                          [("fin", c.c("fin")), ("ex", c.c("ex"))])
                for tt in range(NT):
                    pse = psE if tt % 2 == 0 else psF
                    for ci in range(NCI):
                        c.op("tensor", lambda e, tt=tt, ci=ci, pse=pse: e.matmul(
                            pse[:, 0:128], h_fm[:, ci, tt * P:(tt + 1) * P],
                            wpair[:, ci, 2, :],
                            start=(ci == 0), stop=(ci == NCI - 1)),
                            waits=([("prjv", max(0, c.c("prjv") - 1)),
                                    ("prj", c.c("prj"))]
                                   if ci == 0 else []),
                            incs=([("sc", 1)] if ci == NCI - 1 else []))
                    c.op("vector", lambda e, tt=tt, pse=pse: e.tensor_copy(
                        A_VI[:, tt, 0:64], pse[:, 0:64]),
                        waits=[("sc", c.c("sc"))], incs=[])
                    c.op("vector", lambda e, tt=tt, pse=pse: e.tensor_copy(
                        A_VI[:, tt, 65:129], pse[:, 64:128]),
                        incs=[("prjv", 1)])
                prod_v = c.c("prjv")
                if DEBUG_DUMPS and pr == 0:
                    c.op("sync", lambda e: e.dma_start(out=dbg_q, in_=A_Qp),
                         waits=[("prjv", prod_v), ("prj", c.c("prj")),
                                ("aux2", c.c("aux2"))],
                         incs=[("out", 16)])
                    c.op("sync", lambda e: e.dma_start(out=dbg_k, in_=A_Kp),
                         incs=[("out", 16)])
                    c.op("sync", lambda e: e.dma_start(
                        out=dbg_vi, in_=A_VI.rearrange("p a b -> p (a b)")),
                        incs=[("out", 16)])
                    c.op("sync", lambda e: e.dma_start(out=dbg_sm, in_=smalls[:]),
                         incs=[("out", 16)])
                    c.op("sync", lambda e: e.dma_start(
                        out=dbg_h, in_=h_fm[:, 0, :]),
                        incs=[("out", 16)])
                for qc in range(8):
                    q_sl = slice(qc * 512, (qc + 1) * 512)
                    nkc = 4 * (qc + 1)
                    for kc in range(nkc):
                        sc_ps = psAB if kc % 2 == 0 else psCD
                        diag = kc >= 4 * qc
                        first = (qc == 0 and kc == 0)
                        c.op("tensor", lambda e, kc=kc, sc_ps=sc_ps, q_sl=q_sl: e.matmul(
                            sc_ps[:, 0:512], A_Kp[0:64, kc * P:(kc + 1) * P],
                            A_Qp[0:64, q_sl], start=True, stop=True),
                            waits=([("ex", max(0, c.c("ex") - 1))] +
                                   ([("prjv", prod_v), ("aux2", c.c("aux2")),
                                     ("xin", c.c("xin")), ("lns", c.c("lns")),
                                     ("lnv", c.c("lnv"))]
                                    if first else [])),
                            incs=[])
                        c.op("tensor", lambda e, kc=kc, sc_ps=sc_ps, q_sl=q_sl: e.matmul(
                            sc_ps[:, 512:1024], A_Kp[64:128, kc * P:(kc + 1) * P],
                            A_Qp[64:128, q_sl], start=True, stop=True,
                            tile_position=(64, 0)),
                            incs=[("sc", 1)])
                        pt = A_pt[:, (kc % 2) * 1024:(kc % 2 + 1) * 1024]
                        c.op("scalar", lambda e, sc_ps=sc_ps, pt=pt: e.activation(
                            out=pt, in_=sc_ps[:], func=AF.Exp, scale=0.125,
                            bias=smalls[:, 4:5]),
                            waits=[("sc", c.c("sc")),
                                   ("pv", max(0, c.c("pv") - 1)),
                                   ("ms", c.c("ms"))],
                            incs=[("ex", 1)])
                        if diag:
                            m = kc - 4 * qc
                            c.op("vector", lambda e, m=m, pt=pt: e.tensor_mul(
                                pt[:, 0:512], pt[:, 0:512], mask01[:, m]),
                                waits=[("ex", c.c("ex"))], incs=[])
                            c.op("vector", lambda e, m=m, pt=pt: e.tensor_mul(
                                pt[:, 512:1024], pt[:, 512:1024], mask01[:, m]),
                                incs=[("ms", 1)])
                        pvw = ([("ms", c.c("ms"))] if diag else
                               [("ex", c.c("ex"))])
                        c.op("tensor", lambda e, kc=kc, pt=pt, nkc=nkc: e.matmul(
                            psE[0:65, :], A_VI[:, kc, 0:65], pt[:, 0:512],
                            start=(kc == 0), stop=(kc == nkc - 1)),
                            waits=pvw + ([("fin", c.c("fin")),
                                          ("aof", c.c("aof"))] if kc == 0 else []),
                            incs=[])
                        c.op("tensor", lambda e, kc=kc, pt=pt, nkc=nkc: e.matmul(
                            psF[0:65, :], A_VI[:, kc, 65:130], pt[:, 512:1024],
                            start=(kc == 0), stop=(kc == nkc - 1)),
                            incs=[("pv", 1)])
                    c.op("vector", lambda e: e.reciprocal(lrow[64:65, 0:512],
                                                          psE[64:65, :]),
                         waits=[("pv", c.c("pv"))], incs=[("fin", 1)])
                    c.op("vector", lambda e: e.reciprocal(lrow[64:65, 512:1024],
                                                          psF[64:65, :]),
                         incs=[("fin", 1)])
                    c.op("tensor", lambda e: e.matmul(psS[:, 0:512],
                                                      cmask[64:65, 0, :],
                                                      lrow[64:65, 0:512],
                                                      start=True, stop=False),
                         waits=[("fin", c.c("fin")), ("aux2", c.c("aux2"))],
                         incs=[])
                    c.op("tensor", lambda e: e.matmul(psS[:, 0:512],
                                                      cmask[64:65, 1, :],
                                                      lrow[64:65, 512:1024],
                                                      start=False, stop=True),
                         incs=[("aux", 1)])
                    c.op("scalar", lambda e: e.copy(rlbuf[:], psS[:, 0:512]),
                         waits=[("aux", c.c("aux"))], incs=[("aux2", 1)])
                    c.op("vector", lambda e, pr=pr, q_sl=q_sl: e.tensor_tensor(
                        ao_fm[0:64, pr, q_sl], psE[0:64, :], rlbuf[0:64, :],
                        ALU.mult),
                        waits=[("aux2", c.c("aux2"))], incs=[("fin", 1)])
                    c.op("vector", lambda e, pr=pr, q_sl=q_sl: e.tensor_tensor(
                        ao_fm[64:128, pr, q_sl], psF[0:64, :], rlbuf[64:128, :],
                        ALU.mult),
                        incs=[("aof", 1)])
            aof_all = c.c("aof")
            if DEBUG_DUMPS:
                c.op("sync", lambda e: e.dma_start(
                    out=dbg_ao, in_=ao_fm[:, 0, :]),
                    waits=[("aof", aof_all)], incs=[("out", 16)])
            c.op("gpsimd", lambda e: e.dma_start(
                out=A_wproj[:], in_=wproj.rearrange("(ci p) j -> p ci j", p=P)),
                waits=[("pv", c.c("pv")), ("aux", c.c("aux"))], incs=[("wa", 16)])
            wpl = c.c("wa")
            for tt in range(NT):
                par = tt % 2
                c.op("sync", lambda e, tt=tt: e.dma_start(
                    out=x_tile[:], in_=x0[tt * P:(tt + 1) * P, :]),
                    waits=[("aof", aof_all), ("res", c.c("res")),
                           ("lnv", c.c("lnv")), ("lns", c.c("lns")),
                           ("xst", c.c("xst")), ("ex", c.c("ex")),
                           ("ms", c.c("ms"))],
                    incs=[("xin", 16)])
                xinw = c.c("xin")
                for jh in range(2):
                    pse = psE if jh == 0 else psF
                    for ci in range(NCI):
                        c.op("tensor", lambda e, tt=tt, jh=jh, ci=ci, pse=pse: e.matmul(
                            pse[:], ao_fm[:, ci, tt * P:(tt + 1) * P],
                            A_wproj[:, ci, jh * 512:(jh + 1) * 512],
                            start=(ci == 0), stop=(ci == NCI - 1)),
                            waits=([("aof", aof_all), ("wa", wpl),
                                    ("res", max(0, c.c("res") - 1)),
                                    ("fin", c.c("fin"))]
                                   if ci == 0 else []),
                            incs=([("yev", 1)] if ci == NCI - 1 else []))
                    c.op("vector", lambda e, jh=jh, pse=pse: e.tensor_add(
                        x_tile[:, jh * 512:(jh + 1) * 512],
                        x_tile[:, jh * 512:(jh + 1) * 512], pse[:]),
                        waits=[("yev", c.c("yev")), ("xin", xinw)],
                        incs=[("res", 1)])
                c.op("sync", lambda e, tt=tt: e.dma_start(
                    out=xA[tt * P:(tt + 1) * P, :], in_=x_tile[:]),
                    waits=[("res", c.c("res"))], incs=[("xst", 16)])
                if "B" in STAGES:
                    emit_ln(c, x_tile[:], par, [("res", c.c("res"))],
                            dest_fm_ap=h_fm[:, :, tt * P:(tt + 1) * P],
                            pst=(psCD[:].bitcast(BF16) if tt % 2 else None))

        def stage_ffn(c, which):
            full = which == "B"
            ntile = NT if full else NTH
            hsrc = h_fm if full else h4_fm
            xsrc = xA if full else xC
            if not full:
                for qh in range(2):
                    c.op("sync", lambda e, qh=qh: e.dma_start(
                        out=acc_v[:, qh * 8:(qh + 1) * 8, :],
                        in_=xC[qh * 1024:(qh + 1) * 1024, :].rearrange(
                            "(tt p) c -> p tt c", p=P)),
                        waits=([("acc", c.c("acc")), ("xst", c.c("xst"))]
                               if qh == 0 else []),
                        incs=[("xin2", 16)])
                    for tt in range(8):
                        gt = qh * 8 + tt
                        emit_ln(c, acc_v[:, gt, :], gt % 2,
                                [("xin2", c.c("xin2"))],
                                dest_fm_ap=h4_fm[:, :, gt * P:(gt + 1) * P],
                                pst=(psCD[:].bitcast(BF16) if gt % 2 else None))
            hfm_need = c.c("hfm")
            for tt in range(ntile):
                for ci in range(NCI):
                    c.op("tensor", lambda e, tt=tt, ci=ci: e.matmul(
                        psS[:, 0:NE], hsrc[:, ci, tt * P:(tt + 1) * P],
                        gw_sb[:, ci, :],
                        start=(ci == 0), stop=(ci == NCI - 1)),
                        waits=([("hfm", hfm_need), ("wl", 16),
                                ("rt", c.c("rt"))] if ci == 0 else []),
                        incs=([("sc", 1)] if ci == NCI - 1 else []))
                c.op("vector", lambda e: e.reduce_max(stat[:, 0, 0:1], psS[:, 0:NE],
                                                      axis=AX),
                     waits=[("sc", c.c("sc"))], incs=[("rt", 1)])
                c.op("vector", lambda e: e.tensor_scalar_mul(stat[:, 0, 1:2],
                                                             stat[:, 0, 0:1], -1.0),
                     waits=[("rt", c.c("rt"))], incs=[("rt", 1)])
                c.op("scalar", lambda e: e.activation(out=probs[:], in_=psS[:, 0:NE],
                                                      func=AF.Exp,
                                                      bias=stat[:, 0, 1:2]),
                     waits=[("rt", c.c("rt"))], incs=[("rt", 1)])
                c.op("vector", lambda e: e.max(out=m8[:], in_=probs[:]),
                     waits=[("rt", c.c("rt"))], incs=[("rt", 1)])
                c.op("vector", lambda e: e.tensor_scalar(pm[:], probs[:],
                                                         m8[:, 7:8], None,
                                                         op0=ALU.is_ge),
                     waits=[("rt", c.c("rt"))], incs=[("rt", 1)])
                c.op("vector", lambda e: e.tensor_mul(pm[:], pm[:], probs[:]),
                     waits=[("rt", c.c("rt"))], incs=[("rt", 1)])
                c.op("vector", lambda e: e.reduce_sum(stat[:, 0, 2:3], pm[:], axis=AX),
                     waits=[("rt", c.c("rt"))], incs=[("rt", 1)])
                c.op("vector", lambda e: e.reciprocal(stat[:, 0, 3:4], stat[:, 0, 2:3]),
                     waits=[("rt", c.c("rt"))], incs=[("rt", 1)])
                c.op("vector", lambda e, tt=tt: e.tensor_scalar_mul(
                    comb[:, tt, :], pm[:], stat[:, 0, 3:4]),
                    waits=[("rt", c.c("rt"))], incs=[("rt", 1)])
            rt_done = c.c("rt")
            nhalf = 2 if full else 1
            for hf in range(nhalf):
                tok0 = hf * TH
                if full:
                    c.op("sync", lambda e, tok0=tok0: e.dma_start(
                        out=acc_v[:], in_=xsrc[tok0:tok0 + TH, :].rearrange(
                            "(tt p) c -> p tt c", p=P)),
                        waits=[("acc", c.c("acc")), ("xst", c.c("xst"))],
                        incs=[("xin2", 16)])
                accw = c.c("xin2")
                for e_i in range(NE):
                    par = e_i % 2
                    c.op("gpsimd", lambda e, e_i=e_i, par=par: e.dma_start(
                        out=B_guw[:, par], in_=guwT[e_i].rearrange(
                            "(ci p) j -> p ci j", p=P)),
                        waits=[("wf", max(0, c.c("wf") - 2))],
                        incs=[("wl", 16)])
                    guw_rdy = c.c("wl")
                    c.op("gpsimd", lambda e, e_i=e_i: e.dma_start(
                        out=B_dw[:], in_=dwT[e_i].rearrange(
                            "(ci p) j -> p ci j", p=P)),
                        waits=[("dn", c.c("dn"))],
                        incs=[("wl", 16)])
                    dw_rdy = c.c("wl")
                    for tc in range(4):
                        tsl = slice(tok0 + tc * 512, tok0 + (tc + 1) * 512)
                        for guj in range(8):
                            bank = [psAB[:, 0:512], psAB[:, 512:1024],
                                    psCD[:, 0:512], psCD[:, 512:1024]][guj % 4]
                            bw = ([("guv", c.c("guv"))] if guj < 4
                                  else [("gus", c.c("gus"))])
                            for ci in range(NCI):
                                c.op("tensor", lambda e, guj=guj, ci=ci, bank=bank, tsl=tsl, par=par: e.matmul(
                                    bank, B_guw[:, par, ci, guj * P:(guj + 1) * P],
                                    hsrc[:, ci, tsl],
                                    start=(ci == 0), stop=(ci == NCI - 1)),
                                    waits=(([("wl", guw_rdy)] + bw) if ci == 0 else []),
                                    incs=([("gu", 1)] if ci == NCI - 1 else []))
                            if guj < 4:
                                c.op("scalar", lambda e, guj=guj, bank=bank: e.activation(
                                    out=B_sg[:, guj, :], in_=bank, func=AF.Silu),
                                    waits=[("gu", c.c("gu")),
                                           ("guv", c.c("guv"))],
                                    incs=[("gus", 1)])
                            else:
                                c.op("vector", lambda e, guj=guj, bank=bank: e.tensor_tensor(
                                    out=B_t[:, guj - 4, :], in0=bank,
                                    in1=B_sg[:, guj - 4, :], op=ALU.mult),
                                    waits=[("gu", c.c("gu")),
                                           ("gus", c.c("gus")),
                                           ("dn", c.c("dn")),
                                           ("ms", c.c("ms"))],
                                    incs=[("guv", 1)])
                        guv_now = c.c("guv")
                        for st4 in range(4):
                            gt = tok0 // P + tc * 4 + st4
                            for jh in range(2):
                                pse = psE if (st4 * 2 + jh) % 2 == 0 else psF
                                for ich in range(4):
                                    c.op("tensor", lambda e, st4=st4, jh=jh, ich=ich, pse=pse: e.matmul(
                                        pse[:], B_t[:, ich, st4 * P:(st4 + 1) * P],
                                        B_dw[:, ich, jh * 512:(jh + 1) * 512],
                                        start=(ich == 0), stop=(ich == 3)),
                                        waits=([("guv", guv_now), ("wl", dw_rdy),
                                                ("acc", max(0, c.c("acc") - 1))]
                                               if ich == 0 else []),
                                        incs=([("dn", 1)] if ich == 3 else []))
                                c.op("vector", lambda e, gt=gt, jh=jh, pse=pse, e_i=e_i, hf=hf: e.scalar_tensor_tensor(
                                    out=acc_v[:, gt - hf * NTH, jh * 512:(jh + 1) * 512],
                                    in0=pse[:], scalar=comb[:, gt, e_i:e_i + 1],
                                    in1=acc_v[:, gt - hf * NTH, jh * 512:(jh + 1) * 512],
                                    op0=ALU.mult, op1=ALU.add),
                                    waits=[("dn", c.c("dn")), ("xin2", accw),
                                           ("rt", rt_done)],
                                    incs=[("acc", 1)])
                    c.op("tensor", lambda e: e.nop(), incs=[("wf", 1)])
                acc_done = c.c("acc")
                if full:
                    for tt in range(NTH):
                        c.op("sync", lambda e, tt=tt, tok0=tok0: e.dma_start(
                            out=xB[tok0 + tt * P: tok0 + (tt + 1) * P, :],
                            in_=acc_v[:, tt, :]),
                            waits=[("acc", acc_done)], incs=[("xst", 16)])
                else:
                    for tt in range(NTH):
                        c.op("sync", lambda e, tt=tt: e.dma_start(
                            out=y_out[tt * P:(tt + 1) * P, :], in_=acc_v[:, tt, :]),
                            waits=[("acc", acc_done)], incs=[("out", 16)])

        def stage_C(c):
            c.op("sync", lambda e: e.dma_start(
                out=xQ[:, :], in_=roff(c, xB[0:TH, :])),
                waits=[("xst", c.c("xst"))], incs=[("xin2", 16)])
            xq_rdy = c.c("xin2")
            if C_SUB == "xq":
                return
            c.op("gpsimd", lambda e: e.dma_start(
                out=C_wk[:, :, :], in_=wkT.rearrange("(ci p) j -> p ci j", p=P)),
                waits=[("acc", c.c("acc"))], incs=[("wa", 16)])
            c.op("gpsimd", lambda e: e.dma_start(
                out=C_wv[:, :, :], in_=wvT.rearrange("(ci p) j -> p ci j", p=P)),
                incs=[("wa", 16)])
            wrdy = c.c("wa")
            for jh in range(2):
                c.op("sync", lambda e, jh=jh: e.dma_start(
                    out=rowslot[:], in_=brow_h[2:3, jh * 512:(jh + 1) * 512]),
                    waits=[("aux", c.c("aux")), ("xin2", c.c("xin2"))],
                    incs=[("ld", 16)])
                c.op("tensor", lambda e: e.matmul(
                    psS[:, 0:512], allones[0:1, :], rowslot[:],
                    start=True, stop=True),
                    waits=[("ld", c.c("ld")), ("aux2", c.c("aux2")),
                           ("kv", c.c("kv")), ("ch", c.c("ch")),
                           ("fin", c.c("fin"))],
                    incs=[("aux", 1)])
                c.op("scalar", lambda e, jh=jh: e.copy(
                    C_bvb[:, jh * 512:(jh + 1) * 512], psS[:, 0:512]),
                    waits=[("aux", c.c("aux"))], incs=[("aux2", 1)])
            bvb_rdy = c.c("aux2")
            for tc in range(8):
                for st4 in range(4):
                    tt = tc * 4 + st4
                    par = tt % 2
                    c.op("sync", lambda e, tt=tt: e.dma_start(
                        out=x_tile[:], in_=xB[tt * P:(tt + 1) * P, :]),
                        waits=[("lnv", c.c("lnv")), ("lns", c.c("lns")),
                               ("xst", c.c("xst"))],
                        incs=[("xin", 16)])
                    emit_ln(c, x_tile[:], par, [("xin", c.c("xin"))],
                            dest_fm_ap=hq_v[:, :, st4 * P:(st4 + 1) * P])
                hq_rdy = c.c("hfm")
                for jch in range(8):
                    pse = psE if jch % 2 == 0 else psF
                    for ci in range(NCI):
                        c.op("tensor", lambda e, jch=jch, ci=ci, pse=pse: e.matmul(
                            pse[:], C_wk[:, ci, jch * P:(jch + 1) * P],
                            hq_v[:, ci, :],
                            start=(ci == 0), stop=(ci == NCI - 1)),
                            waits=([("hfm", hq_rdy), ("wa", wrdy),
                                    ("kv", max(0, c.c("kv") - 1)),
                                    ("ksq", max(0, c.c("ksq") - 1))]
                                   if ci == 0 else []),
                            incs=([("sc", 1)] if ci == NCI - 1 else []))
                    c.op("vector", lambda e, jch=jch, tc=tc, pse=pse: e.tensor_scalar(
                        out=K_fm[:, jch, tc * 512:(tc + 1) * 512],
                        in0=pse[:], scalar1=bias_sb[:, 1, jch:jch + 1],
                        scalar2=None, op0=ALU.add),
                        waits=[("sc", c.c("sc")), ("ld", 16)],
                        incs=[("kv", 1)])
                    c.op("scalar", lambda e, jch=jch, tc=tc: e.activation(
                        out=rlbuf[:].bitcast(BF16)[:, 0:512],
                        in_=K_fm[:, jch, tc * 512:(tc + 1) * 512],
                        func=AF.Square),
                        waits=[("qn", c.c("qn")), ("kv", c.c("kv"))],
                        incs=[("ksq", 1)])
                    c.op("tensor", lambda e, jch=jch: e.matmul(
                        psS[0:1, 0:512], onesr_b[:],
                        rlbuf[:].bitcast(BF16)[:, 0:512],
                        start=(jch == 0), stop=(jch == 7)),
                        waits=[("ksq", c.c("ksq"))] +
                              ([("ch", c.c("ch")), ("fin", c.c("fin"))]
                               if jch == 0 else []),
                        incs=[("qn", 1)])
                c.op("vector", lambda e: e.tensor_copy(
                    rowslot2[:], psS[0:1, 0:512]),
                    waits=[("qn", c.c("qn"))], incs=[("ch", 1)])
                for q4 in range(4):
                    c.op("tensor", lambda e, q4=q4: e.transpose(
                        psS[:, 256:257],
                        rowslot2[0:1, q4 * P:(q4 + 1) * P],
                        id_f32[0:1, 0:1]),
                        waits=[("ch", c.c("ch")), ("fin", c.c("fin"))],
                        incs=[("aux", 1)])
                    col = tc * 4 + q4
                    c.op("vector", lambda e, col=col: e.tensor_copy(
                        norms[:, col:col + 1], psS[:, 256:257]),
                        waits=[("aux", c.c("aux"))], incs=[("fin", 1)])
                for st4 in range(4):
                    tt = tc * 4 + st4
                    for jh in range(2):
                        pse = psE if (st4 * 2 + jh) % 2 == 0 else psF
                        for ci in range(NCI):
                            c.op("tensor", lambda e, st4=st4, jh=jh, ci=ci, pse=pse: e.matmul(
                                pse[:], hq_v[:, ci, st4 * P:(st4 + 1) * P],
                                C_wv[:, ci, jh * 512:(jh + 1) * 512],
                                start=(ci == 0), stop=(ci == NCI - 1)),
                                waits=([("kv", max(0, c.c("kv") - 1))]
                                       if ci == 0 else []),
                                incs=([("sc", 1)] if ci == NCI - 1 else []))
                        c.op("vector", lambda e, tt=tt, jh=jh, pse=pse: e.tensor_add(
                            V_tm[:, tt, jh * 512:(jh + 1) * 512], pse[:],
                            C_bvb[:, jh * 512:(jh + 1) * 512]),
                            waits=[("sc", c.c("sc")), ("aux2", bvb_rdy)],
                            incs=[("kv", 1)])
            c.op("scalar", lambda e: e.activation(out=norms[:, 32:64],
                                                  in_=norms[:, 0:32], func=AF.Sqrt,
                                                  bias=consts[:, 2:3]),
                 waits=[("fin", c.c("fin"))], incs=[("chs", 1)])
            c.op("vector", lambda e: e.tensor_scalar_add(norms[:, 32:64],
                                                         norms[:, 32:64], 1e-8),
                 waits=[("chs", c.c("chs"))], incs=[("ch", 1)])
            c.op("vector", lambda e: e.reciprocal(norms[:, 32:64], norms[:, 32:64]),
                 waits=[("ch", c.c("ch"))], incs=[("ch", 1)])
            c.op("vector", lambda e: e.tensor_scalar_mul(norms[:, 32:64],
                                                         norms[:, 32:64], PI),
                 waits=[("ch", c.c("ch"))], incs=[("ch", 1)])
            colmax_scalar(c, norms[:, 0:32], P, smalls[0:1, 8:9],
                          [("ch", c.c("ch"))])
            c.op("gpsimd", lambda e: e.dma_start(
                out=C_wsl[:, :, :],
                in_=wqT.rearrange("(ci p) j -> p ci j", p=P)[:, :, 0:512]),
                waits=[("acc", c.c("acc"))], incs=[("wa", 16)])
            c.op("gpsimd", lambda e: e.dma_start(
                out=C_wq2[:, :, :],
                in_=wqT.rearrange("(ci p) j -> p ci j", p=P)[:, :, 512:1024]),
                waits=[("ms", c.c("ms")), ("acc", c.c("acc"))],
                incs=[("wa", 16)])
            wq_rdy = c.c("wa")
            for tc in range(4):
                for st4 in range(4):
                    tt = tc * 4 + st4
                    par = tt % 2
                    c.op("sync", lambda e, tt=tt: e.dma_start(
                        out=x_tile[:],
                        in_=xQ[tt * P:(tt + 1) * P, :]),
                        waits=[("lnv", c.c("lnv")), ("lns", c.c("lns")),
                               ("xin2", xq_rdy)],
                        incs=[("xin", 16)])
                    emit_ln(c, x_tile[:], par, [("xin", c.c("xin"))],
                            dest_fm_ap=hq_v[:, :, st4 * P:(st4 + 1) * P])
                hq_rdy = c.c("hfm")
                for jch in range(8):
                    wsrc = C_wsl if jch < 4 else C_wq2
                    jj = jch % 4
                    pse = psE if jch % 2 == 0 else psF
                    for ci in range(NCI):
                        c.op("tensor", lambda e, jj=jj, ci=ci, pse=pse, wsrc=wsrc: e.matmul(
                            pse[:], wsrc[:, ci, jj * P:(jj + 1) * P],
                            hq_v[:, ci, :],
                            start=(ci == 0), stop=(ci == NCI - 1)),
                            waits=([("hfm", hq_rdy), ("wa", wq_rdy),
                                    ("kv", max(0, c.c("kv") - 1)),
                                    ("ksq", max(0, c.c("ksq") - 1))]
                                   if ci == 0 else []),
                            incs=([("sc", 1)] if ci == NCI - 1 else []))
                    c.op("vector", lambda e, jch=jch, tc=tc, pse=pse: e.tensor_scalar(
                        out=C_Q[:, jch, tc * 512:(tc + 1) * 512],
                        in0=pse[:], scalar1=bias_sb[:, 0, jch:jch + 1],
                        scalar2=None, op0=ALU.add),
                        waits=[("sc", c.c("sc"))], incs=[("kv", 1)])
                    c.op("scalar", lambda e, jch=jch, tc=tc: e.activation(
                        out=rlbuf[:].bitcast(BF16)[:, 0:512],
                        in_=C_Q[:, jch, tc * 512:(tc + 1) * 512],
                        func=AF.Square),
                        waits=[("qn", c.c("qn")), ("kv", c.c("kv"))],
                        incs=[("ksq", 1)])
                    c.op("tensor", lambda e, jch=jch: e.matmul(
                        psS[0:1, 0:512], onesr_b[:],
                        rlbuf[:].bitcast(BF16)[:, 0:512],
                        start=(jch == 0), stop=(jch == 7)),
                        waits=[("ksq", c.c("ksq"))] +
                              ([("ch", c.c("ch")), ("fin", c.c("fin"))]
                               if jch == 0 else []),
                        incs=[("qn", 1)])
                c.op("vector", lambda e: e.tensor_copy(rowslot2[:],
                                                       psS[0:1, 0:512]),
                     waits=[("qn", c.c("qn"))], incs=[("ch", 1)])
                for q4 in range(4):
                    c.op("tensor", lambda e, q4=q4: e.transpose(
                        psS[:, 256:257], rowslot2[0:1, q4 * P:(q4 + 1) * P],
                        id_f32[0:1, 0:1]),
                        waits=[("ch", c.c("ch")), ("fin", c.c("fin"))],
                        incs=[("aux", 1)])
                    col = 64 + tc * 4 + q4
                    c.op("vector", lambda e, col=col: e.tensor_copy(
                        norms[:, col:col + 1], psS[:, 256:257]),
                        waits=[("aux", c.c("aux"))], incs=[("fin", 1)])
            q_done = c.c("kv")
            if C_SUB == "q":
                return
            c.op("scalar", lambda e: e.activation(out=norms[:, 80:96],
                                                  in_=norms[:, 64:80], func=AF.Sqrt,
                                                  bias=consts[:, 2:3]),
                 waits=[("fin", c.c("fin"))], incs=[("chs", 1)])
            c.op("vector", lambda e: e.tensor_scalar_add(norms[:, 80:96],
                                                         norms[:, 80:96], 1e-8),
                 waits=[("chs", c.c("chs"))], incs=[("ch", 1)])
            c.op("vector", lambda e: e.reciprocal(norms[:, 80:96], norms[:, 80:96]),
                 waits=[("ch", c.c("ch"))], incs=[("ch", 1)])
            colmax_scalar(c, norms[:, 64:80], P, smalls[0:1, 9:10],
                          [("ch", c.c("ch"))])
            c.op("vector", lambda e: e.tensor_mul(smalls[0:1, 10:11],
                                                  smalls[0:1, 8:9],
                                                  smalls[0:1, 9:10]),
                 waits=[("fin", c.c("fin"))], incs=[("fin", 1)])
            c.op("scalar", lambda e: e.activation(out=smalls[0:1, 11:12],
                                                  in_=smalls[0:1, 10:11],
                                                  func=AF.Sqrt,
                                                  bias=consts[0:1, 2:3]),
                 waits=[("fin", c.c("fin"))], incs=[("chs", 1)])
            c.op("vector", lambda e: e.tensor_scalar_mul(smalls[0:1, 12:13],
                                                         smalls[0:1, 11:12],
                                                         -1.0 / 32.0),
                 waits=[("chs", c.c("chs"))], incs=[("fin", 1)])
            bcast_col(c, smalls[0:1, 12:13], smalls[:, 13:14],
                      [("fin", c.c("fin"))])
            for tq in range(NTH):
                c.op("vector", lambda e, tq=tq: e.tensor_scalar(
                    sqs[:, 0:P], id_f32[:], norms[:, 80 + tq:81 + tq], None,
                    op0=ALU.mult),
                    waits=[("ch", c.c("ch")), ("lns", c.c("lns")),
                           ("aux", c.c("aux"))],
                    incs=[("ch", 1)])
                c.op("tensor", lambda e, tq=tq: e.matmul(
                    psS[:, 0:128], allones[:], sqs[:, 0:P],
                    start=True, stop=True),
                    waits=[("ch", c.c("ch")), ("aux2", c.c("aux2"))],
                    incs=[("aux", 1)])
                c.op("scalar", lambda e, tq=tq: e.copy(
                    rqb[:, tq * P:(tq + 1) * P], psS[:, 0:128]),
                    waits=[("aux", c.c("aux"))], incs=[("aux2", 1)])
            rqb_done = c.c("aux2")
            for bh in range(2):
                c.op("sync", lambda e, bh=bh: e.dma_start(
                    out=rowslot[:], in_=brow_h[3:4, bh * 512:(bh + 1) * 512]),
                    waits=[("aux", c.c("aux")), ("acc", c.c("acc"))],
                    incs=[("ld", 16)])
                c.op("tensor", lambda e: e.matmul(
                    psS[:, 0:512], allones[0:1, :], rowslot[:],
                    start=True, stop=True),
                    waits=[("ld", c.c("ld")), ("aux2", c.c("aux2"))],
                    incs=[("aux", 1)])
                c.op("scalar", lambda e, bh=bh: e.copy(
                    C_bob[:, bh * 512:(bh + 1) * 512], psS[:, 0:512]),
                    waits=[("aux", c.c("aux")), ("rt", c.c("rt"))],
                    incs=[("aux2", 1)])
            bob_rdy = c.c("aux2")
            if C_SUB == "pre":
                return
            for qc in range(8):
                q_sl = slice(qc * 256, (qc + 1) * 256)
                for kc in range(NT):
                    sc_ps = psE if kc % 2 == 0 else psF
                    first = (qc == 0 and kc == 0)
                    for ci in range(NCI):
                        qkw = []
                        if ci == 0:
                            qkw = [("ch", max(0, c.c("ch") - 3))]
                            if kc <= 1:
                                qkw += [("res", c.c("res"))]
                            if first:
                                qkw += [("kv", q_done), ("aux2", bob_rdy),
                                        ("ksq", c.c("ksq"))]
                        c.op("tensor", lambda e, kc=kc, ci=ci, sc_ps=sc_ps, q_sl=q_sl: e.matmul(
                            sc_ps[:, 0:256], K_fm[:, ci, kc * P:(kc + 1) * P],
                            C_Q[:, ci, q_sl],
                            start=(ci == 0), stop=(ci == NCI - 1)),
                            waits=qkw,
                            incs=([("sc", 1)] if ci == NCI - 1 else []))
                    c.op("vector", lambda e, sc_ps=sc_ps, q_sl=q_sl: e.tensor_tensor(
                        sqs[:, 0:256], sc_ps[:, 0:256], rqb[:, q_sl], ALU.mult),
                        waits=[("sc", c.c("sc")), ("chs", c.c("chs")),
                               ("ex", c.c("ex"))],
                        incs=[("ch", 1)])
                    c.op("scalar", lambda e, kc=kc: e.activation(
                        out=sqs[:, 256:512], in_=sqs[:, 0:256], func=AF.Abs,
                        scale=norms[:, 32 + kc:33 + kc]),
                        waits=[("ch", c.c("ch"))], incs=[("chs", 1)])
                    c.op("vector", lambda e: e.tensor_scalar_min(
                        sqs[:, 256:512], sqs[:, 256:512], PI),
                        waits=[("chs", c.c("chs"))], incs=[("ch", 1)])
                    c.op("scalar", lambda e: e.activation(
                        out=sqs[:, 256:512], in_=sqs[:, 256:512], func=AF.Sin,
                        scale=-1.0, bias=consts[:, 1:2]),
                        waits=[("ch", c.c("ch"))], incs=[("chs", 1)])
                    c.op("vector", lambda e, sc_ps=sc_ps: e.scalar_tensor_tensor(
                        out=sqs[:, 0:256], in0=sc_ps[:, 0:256], scalar=1.0 / 32.0,
                        in1=sqs[:, 256:512], op0=ALU.mult, op1=ALU.mult),
                        waits=[("chs", c.c("chs"))], incs=[("ch", 1)])
                    pt = C_pt[:, (kc % 2) * 256:(kc % 2 + 1) * 256]
                    c.op("scalar", lambda e, pt=pt: e.activation(
                        out=pt, in_=sqs[:, 0:256], func=AF.Exp,
                        bias=smalls[:, 13:14]),
                        waits=[("ch", c.c("ch")),
                               ("pv", max(0, c.c("pv") - 1))],
                        incs=[("ex", 1)])
                    exn = c.c("ex")
                    for sub in range(2):
                        for jjh in range(2):
                            bank = [psAB[:, 0:512], psAB[:, 512:1024],
                                    psCD[:, 0:512], psCD[:, 512:1024]][sub * 2 + jjh]
                            pvw = []
                            if sub == 0 and jjh == 0:
                                pvw = [("ex", exn)]
                                if kc == 0:
                                    pvw += [("fin", c.c("fin")),
                                            ("chs", c.c("chs")),
                                            ("aux", c.c("aux"))]
                            c.op("tensor", lambda e, kc=kc, sub=sub, jjh=jjh, bank=bank, pt=pt: e.matmul(
                                bank, pt[:, sub * P:(sub + 1) * P],
                                V_tm[:, kc, jjh * 512:(jjh + 1) * 512],
                                start=(kc == 0), stop=(kc == NT - 1)),
                                waits=pvw, incs=[])
                    c.op("tensor", lambda e, kc=kc, pt=pt: e.matmul(
                        psS[0:1, 0:256], onesr_b[:], pt[:],
                        start=(kc == 0), stop=(kc == NT - 1)),
                        waits=([("fin", c.c("fin"))] if kc == 0 else []),
                        incs=[("pv", 1)])
                c.op("vector", lambda e: e.reciprocal(rowslot[0:1, 0:256],
                                                      psS[0:1, 0:256]),
                     waits=[("pv", c.c("pv"))],
                     incs=[("fin", 1)])
                for sub in range(2):
                    c.op("tensor", lambda e, sub=sub: e.transpose(
                        psS[:, 300 + sub:301 + sub],
                        rowslot[0:1, sub * P:(sub + 1) * P], id_f32[0:1, 0:1]),
                        waits=[("fin", c.c("fin"))], incs=[("aux", 1)])
                c.op("vector", lambda e: e.tensor_copy(norms[:, 90:92],
                                                       psS[:, 300:302]),
                     waits=[("aux", c.c("aux"))], incs=[("fin", 1)])
                for sub in range(2):
                    for jjh in range(2):
                        bank = [psAB[:, 0:512], psAB[:, 512:1024],
                                psCD[:, 0:512], psCD[:, 512:1024]][sub * 2 + jjh]
                        c.op("scalar", lambda e, sub=sub, jjh=jjh, bank=bank: e.activation(
                            out=h_tm[:, jjh * 512:(jjh + 1) * 512], in_=bank,
                            func=AF.Copy, scale=norms[:, 90 + sub:91 + sub]),
                            waits=[("fin", c.c("fin"))] +
                                  ([("tr", c.c("tr"))] if jjh == 0 else []),
                            incs=[("chs", 1)])
                    for k in range(NCI):
                        c.op("tensor", lambda e, k=k: e.transpose(
                            psT[:, k * P:(k + 1) * P], h_tm[:, k * P:(k + 1) * P],
                            id_bf[:]),
                            waits=([("chs", c.c("chs")), ("hfm", c.c("hfm"))]
                                   if k == 0 else []),
                            incs=([("tr", 1)] if k == NCI - 1 else []))
                    c.op("vector", lambda e, sub=sub: e.tensor_copy(
                        C_aoqc[:, :, sub * P:(sub + 1) * P], psT[:, 0:C]),
                        waits=[("tr", c.c("tr")), ("yev", c.c("yev")),
                               ("ms", c.c("ms"))],
                        incs=[("hfm", 1)])
                aoq_rdy = c.c("hfm")
                for jjh in range(2):
                    c.op("gpsimd", lambda e, jjh=jjh: e.dma_start(
                        out=C_wsl[:, :, :],
                        in_=woT.rearrange("(ci p) j -> p ci j", p=P)
                        [:, :, jjh * 512:(jjh + 1) * 512]),
                        waits=[("yev", c.c("yev")), ("sc", c.c("sc"))]
                              if (qc == 0 and jjh == 0) else
                              [("yev", c.c("yev"))],
                        incs=[("wa", 16)])
                    wo_rdy = c.c("wa")
                    for sub in range(2):
                        gq = qc * 2 + sub
                        c.op("sync", lambda e, gq=gq, sub=sub, jjh=jjh: e.dma_start(
                            out=x_tile[:, sub * 512:(sub + 1) * 512],
                            in_=xQ[gq * P:(gq + 1) * P,
                                   jjh * 512:(jjh + 1) * 512]),
                            waits=[("res", c.c("res")), ("ex", c.c("ex")),
                                   ("ms", c.c("ms"))],
                            incs=[("xin", 16)])
                        xw = c.c("xin")
                        pse = psE if sub == 0 else psF
                        for ci in range(NCI):
                            c.op("tensor", lambda e, sub=sub, jjh=jjh, ci=ci, pse=pse: e.matmul(
                                pse[:], C_aoqc[:, ci, sub * P:(sub + 1) * P],
                                C_wsl[:, ci, :],
                                start=(ci == 0), stop=(ci == NCI - 1)),
                                waits=([("hfm", aoq_rdy), ("wa", wo_rdy),
                                        ("res", max(0, c.c("res") - 2))]
                                       if ci == 0 else []),
                                incs=([("yev", 1)] if ci == NCI - 1 else []))
                        c.op("vector", lambda e, jjh=jjh, pse=pse: e.tensor_add(
                            rlbuf[:], pse[:], C_bob[:, jjh * 512:(jjh + 1) * 512]),
                            waits=[("yev", c.c("yev")), ("ksq", c.c("ksq")),
                                   ("xst", c.c("xst"))],
                            incs=[("res", 1)])
                        c.op("vector", lambda e, sub=sub: e.tensor_add(
                            rlbuf[:], rlbuf[:],
                            x_tile[:, sub * 512:(sub + 1) * 512]),
                            waits=[("xin", xw)],
                            incs=[("res", 1)])
                        c.op("sync", lambda e, gq=gq, jjh=jjh: e.dma_start(
                            out=xC[gq * P:(gq + 1) * P, jjh * 512:(jjh + 1) * 512],
                            in_=rlbuf[:]),
                            waits=[("res", c.c("res"))], incs=[("xst", 16)])

        def emit_tap(c):
            last = STAGES[-1]
            if last == "D":
                c.op("sync", lambda e: (e.wait_ge(c.sems["out"], c.c("out")), None)[1])
                return
            eff = last
            if last == "C" and C_SUB != "full":
                eff = "B"
            src = {"A": xA, "B": xB, "C": xC}[eff]
            for tt in range(NTH):
                if eff in ("A", "B"):
                    inap = roff(c, src[tt * P:(tt + 1) * P, :])
                else:
                    inap = src[tt * P:(tt + 1) * P, :]
                c.op("sync", lambda e, tt=tt, inap=inap: e.dma_start(
                    out=x_tile[:], in_=inap),
                    waits=[("xst", c.c("xst")), ("out", c.c("out"))],
                    incs=[("xin2", 16)])
                c.op("sync", lambda e, tt=tt: e.dma_start(
                    out=y_out[tt * P:(tt + 1) * P, :], in_=x_tile[:]),
                    waits=[("xin2", c.c("xin2"))], incs=[("out", 16)])
            c.op("sync", lambda e: (e.wait_ge(c.sems["out"], c.c("out")), None)[1])

        def program(c):
            emit_consts(c)
            phase_h1(c)
            if "A" in STAGES:
                stage_A(c)
            c.barrier()
            if "B" in STAGES:
                stage_ffn(c, "B")
                c.barrier()
            if "C" in STAGES:
                stage_C(c)
                c.barrier()
            if "D" in STAGES:
                stage_ffn(c, "D")
                c.barrier()
            emit_tap(c)

        with nc.Block() as blk:
            for engname in ENGINES:
                def make_emit(en):
                    def emit(eng):
                        ctx.start(en, eng)
                        program(ctx)
                    return emit
                getattr(blk, engname)(make_emit(engname))
    return nc


_NC_CACHE = None


def _get_nc():
    global _NC_CACHE
    if _NC_CACHE is None:
        _NC_CACHE = build_nc()
    return _NC_CACHE


def kernel(**inputs):
    f = lambda k: np.ascontiguousarray(np.asarray(inputs[k], dtype=np.float32))
    x = f("L0_attn_a_x")
    W_attn = f("L0_attn_a_W_attn")
    W_proj = f("L0_attn_a_W_proj")
    Wq, bq = f("L0_attn_b_Wq"), f("L0_attn_b_bq")
    Wk, bk = f("L0_attn_b_Wk"), f("L0_attn_b_bk")
    Wv, bv = f("L0_attn_b_Wv"), f("L0_attn_b_bv")
    Wo, bo = f("L0_attn_b_Wo"), f("L0_attn_b_bo")
    gate_w = f("L0_ffn_gate_w")
    gate_up_w = f("L0_ffn_gate_up_w")
    down_w = f("L0_ffn_down_w")

    biases = np.ascontiguousarray(np.stack([bq, bk, bv, bo]))
    bias_hc = np.ascontiguousarray(biases.reshape(4, NCI, P).transpose(2, 0, 1))
    gw_hc = np.ascontiguousarray(gate_w.T.reshape(NCI, P, NE).transpose(1, 0, 2))
    wqT = np.ascontiguousarray(Wq.T)
    wkT = np.ascontiguousarray(Wk.T)
    wvT = np.ascontiguousarray(Wv.T)
    woT = np.ascontiguousarray(Wo.T)
    guwT = np.ascontiguousarray(gate_up_w.transpose(0, 2, 1))
    dwT = np.ascontiguousarray(down_w.transpose(0, 2, 1))

    nc = _get_nc()
    in_maps = []
    for core in range(8):
        b = core // 2
        in_maps.append({
            "x0": np.ascontiguousarray(x[b]),
            "wattn": W_attn, "wproj": W_proj,
            "wqT": wqT, "wkT": wkT, "wvT": wvT, "woT": woT,
            "bias_h": bias_hc, "brow_h": biases, "gw_h": gw_hc,
            "guwT": guwT, "dwT": dwT,
        })
    res = run_bass_kernel_spmd(nc, in_maps, core_ids=list(range(8)))
    out = np.empty((4, T, C), dtype=np.float32)
    for core in range(8):
        b, h = core // 2, core % 2
        out[b, h * TH:(h + 1) * TH, :] = res.results[core]["y"]
    return out


# version 05
# speedup vs baseline: 1.1664x; 1.1128x over previous
import math
import dataclasses
import numpy as np
from contextlib import ExitStack

import concourse.bass as bass
import concourse.mybir as mybir
from concourse.bass_utils import run_bass_kernel_spmd

F32 = mybir.dt.float32
BF16 = mybir.dt.bfloat16
AF = mybir.ActivationFunctionType
ALU = mybir.AluOpType
AX = mybir.AxisListType.X

P = 128
T = 4096
TH = 2048
C = 1024
NT = T // P
NTH = TH // P
NCI = C // P
NE = 32
NPAIR = 8
EPS = 1e-5
PI = math.pi

STAGES = "ABCD"
DEBUG_DUMPS = False
C_SUB = "full"

ENGINES = ["sync", "tensor", "vector", "scalar", "gpsimd"]


class Ctx:
    def __init__(self, nc):
        self.nc = nc
        self.sems = {}
        self.counts = {}
        self.cur = None
        self.eng = None
        self.es = None
        self._q0 = None

    def make_sem(self, name):
        h = self.es.enter_context(self.nc.semaphore(name))
        self.sems[name] = h
        self.counts[name] = 0

    def start(self, engname, eng):
        self.cur = engname
        self.eng = eng
        self._q0 = None
        for k in self.counts:
            self.counts[k] = 0

    def c(self, name):
        return self.counts[name]

    def q0row(self):
        if self._q0 is None:
            cid = self.eng.partition_id()
            self._q0 = (cid % 2) * (TH * C)
        return self._q0

    def op(self, engname, fn, waits=(), incs=()):
        if self.cur == engname:
            e = self.eng
            for s, v in waits:
                if v > 0:
                    e.wait_ge(self.sems[s], v)
            inst = fn(e)
            if incs:
                assert inst is not None, "op with incs must return an instruction"
                for s, n in incs:
                    inst.then_inc(self.sems[s], n)
        for s, n in incs:
            self.counts[s] += n

    def barrier(self):
        for en in ENGINES:
            self.op(en, lambda e: e.nop(), incs=[("bar", 1)])
        tgt = self.c("bar")
        for en in ENGINES:
            self.op(en, lambda e, t=tgt: (e.wait_ge(self.sems["bar"], t), None)[1])


def roff(ctx, ap):
    return dataclasses.replace(ap, offset=ap.offset + ctx.q0row())


SEMS = [
    "bar", "cst", "ld", "xin", "xin2", "lnv", "lns", "tr", "hfm",
    "wa", "prj", "prjv", "sc", "ms", "ex", "pv", "fin", "aux", "aux2",
    "aof", "yev", "res", "xst",
    "wl", "wf", "rt", "gu", "guv", "gus", "dn", "acc",
    "kv", "ksq", "qn", "ch", "chs", "l0", "out",
]


def build_nc():
    nc = bass.Bass("TRN2", target_bir_lowering=False, debug=False,
                   detect_race_conditions=False)
    ctx = Ctx(nc)
    es = ExitStack()
    ctx.es = es
    with es:
        for s in SEMS:
            ctx.make_sem(s)

        x0 = nc.dram_tensor("x0", [T, C], F32, kind="ExternalInput").ap()
        wattn = nc.dram_tensor("wattn", [C, 3 * C], F32, kind="ExternalInput").ap()
        wproj = nc.dram_tensor("wproj", [C, C], F32, kind="ExternalInput").ap()
        wqT = nc.dram_tensor("wqT", [C, C], F32, kind="ExternalInput").ap()
        wkT = nc.dram_tensor("wkT", [C, C], F32, kind="ExternalInput").ap()
        wvT = nc.dram_tensor("wvT", [C, C], F32, kind="ExternalInput").ap()
        woT = nc.dram_tensor("woT", [C, C], F32, kind="ExternalInput").ap()
        bias_h = nc.dram_tensor("bias_h", [P, 4, NCI], F32, kind="ExternalInput").ap()
        brow_h = nc.dram_tensor("brow_h", [4, C], F32, kind="ExternalInput").ap()
        gw_h = nc.dram_tensor("gw_h", [P, NCI, NE], F32, kind="ExternalInput").ap()
        guwT = nc.dram_tensor("guwT", [NE, C, C], F32, kind="ExternalInput").ap()
        dwT = nc.dram_tensor("dwT", [NE, 512, C], F32, kind="ExternalInput").ap()
        y_out = nc.dram_tensor("y", [TH, C], F32, kind="ExternalOutput").ap()
        xA = nc.dram_tensor("xA", [T, C], F32).ap()
        xB = nc.dram_tensor("xB", [T, C], F32).ap()
        xC = nc.dram_tensor("xC", [TH, C], F32).ap()
        xQ = nc.dram_tensor("xQ", [TH, C], F32).ap()
        if DEBUG_DUMPS:
            dbg_q = nc.dram_tensor("dbg_q", [P, 4096], BF16, kind="ExternalOutput").ap()
            dbg_k = nc.dram_tensor("dbg_k", [P, 4096], BF16, kind="ExternalOutput").ap()
            dbg_vi = nc.dram_tensor("dbg_vi", [P, 4160], BF16, kind="ExternalOutput").ap()
            dbg_sm = nc.dram_tensor("dbg_sm", [P, 16], F32, kind="ExternalOutput").ap()
            dbg_h = nc.dram_tensor("dbg_h", [P, 4096], BF16, kind="ExternalOutput").ap()
            dbg_ao = nc.dram_tensor("dbg_ao", [P, 4096], BF16, kind="ExternalOutput").ap()

        sb = lambda n, s, d=F32: es.enter_context(nc.sbuf_tensor(n, s, d))
        pool_x = sb("pool_x", [P, NCI * T], BF16)
        pool_y = sb("pool_y", [P, NTH * C], F32)
        pool_z = sb("pool_z", [P, 22528], BF16)
        msks = sb("msks", [P, 4096], BF16)
        x_tile = sb("x_tile", [P, C], F32)
        h_tm = sb("h_tm", [P, C], BF16)
        sqs = sb("sqs", [P, C], F32)
        rqb = sb("rqb", [P, TH], F32)
        rowslot = sb("rowslot", [1, 512], F32)
        rowslot2 = sb("rowslot2", [1, 512], F32)
        rlbuf = sb("rlbuf", [P, 512], F32)
        stat = sb("stat", [P, 2, 8], F32)
        consts = sb("consts", [P, 8], F32)
        bias_sb = sb("bias_sb", [P, 4, NCI], F32)
        gw_sb = sb("gw_sb", [P, NCI, NE], BF16)
        id_bf = sb("id_bf", [P, P], BF16)
        id_f32 = sb("id_f32", [P, P], F32)
        allones = sb("allones", [P, P], F32)
        ones1f = sb("ones1f", [P, 1], F32)
        ones2 = sb("ones2", [P, 2], BF16)
        onesr_b = sb("onesr_b", [P, 1], BF16)
        cmask = sb("cmask", [P, 2, P], F32)
        norms = sb("norms", [P, 96], F32)
        smalls = sb("smalls", [P, 16], F32)
        probs = sb("probs", [P, NE], F32)
        m8 = sb("m8", [P, 8], F32)
        pm = sb("pm", [P, NE], F32)

        ps = lambda n, s, d=F32: es.enter_context(nc.psum_tensor(n, s, d))
        psAB = ps("psAB", [P, 1024])
        psCD = ps("psCD", [P, 1024])
        psE = ps("psE", [P, 512])
        psF = ps("psF", [P, 512])
        psT = ps("psT", [P, 1024], BF16)
        psS = ps("psS", [P, 512])

        h_fm = pool_x[:].rearrange("p (ci t) -> p ci t", ci=NCI)
        K_fm = h_fm
        h4_fm = pool_x[:, 0:NCI * TH].rearrange("p (ci t) -> p ci t", ci=NCI)
        ao_fm = pool_y[:].bitcast(BF16).rearrange("p (ci t) -> p ci t", ci=NCI)
        acc_v = pool_y[:].rearrange("p (tt c) -> p tt c", tt=NTH)
        V_tm = pool_y[:].bitcast(BF16).rearrange("p (tt c) -> p tt c", tt=NT)

        zb = pool_z
        A_Qp = zb[:, 0:4096]
        A_Kp = zb[:, 4096:8192]
        A_VI = zb[:, 8192:8192 + 32 * 130].rearrange("p (tt v) -> p tt v", tt=NT)
        wpair = zb[:, 12352:15424].rearrange("p (ci w k) -> p ci w k", ci=NCI, w=3)
        A_sq = zb[:, 15424:15936]
        A_wproj = zb[:, 0:8192].rearrange("p (ci j) -> p ci j", ci=NCI)
        B_guw = zb[:, 0:16384].rearrange("p (b ci j) -> p b ci j", b=2, ci=NCI)
        B_dw = zb[:, 16384:20480].rearrange("p (ci j) -> p ci j", ci=4)
        B_sg = zb[:, 20480:22528].rearrange("p (g t) -> p g t", g=4)
        C_Q = zb[:, 0:16384].rearrange("p (ci t) -> p ci t", ci=NCI)
        C_wk = zb[:, 0:8192].rearrange("p (ci j) -> p ci j", ci=NCI)
        C_wv = zb[:, 8192:16384].rearrange("p (ci j) -> p ci j", ci=NCI)
        C_wsl = zb[:, 16384:20480].rearrange("p (ci j) -> p ci j", ci=NCI)
        C_wq2 = msks[:, 0:4096].rearrange("p (ci j) -> p ci j", ci=NCI)
        C_bvb = zb[:, 20480:22528].bitcast(F32)

        mask01 = msks[:, 0:2048].rearrange("p (m k) -> p m k", m=4)
        B_t = msks[:, 0:2048].rearrange("p (g t) -> p g t", g=4)
        comb = msks[:, 2048:4096].bitcast(F32).rearrange(
            "p (tt e) -> p tt e", tt=NT)
        C_aoqc = msks[:, 0:2048].rearrange("p (ci q) -> p ci q", ci=NCI)
        C_bob = msks[:, 2048:4096].bitcast(F32)

        A_pt = x_tile[:].bitcast(BF16)
        A_pt2 = sqs[:].bitcast(BF16)
        C_pt = sqs[:].bitcast(BF16)[:, 1024:1536]
        hq_v = rqb[:].bitcast(BF16).rearrange("p (ci t) -> p ci t", ci=NCI)
        lrow = rqb

        def emit_consts(c):
            def base(e):
                e.memset(consts[:, 0:1], EPS)
                e.memset(consts[:, 1:2], PI / 2)
                e.memset(consts[:, 2:3], 0.0)
                e.memset(allones[:], 1.0)
                e.memset(ones1f[:], 1.0)
                e.memset(onesr_b[:], 1.0)
                e.memset(ones2[0:64, 0:1], 1.0)
                e.memset(ones2[64:128, 0:1], 0.0)
                e.memset(ones2[0:64, 1:2], 0.0)
                e.memset(ones2[64:128, 1:2], 1.0)
                e.memset(cmask[0:64, :, :], 0.0)
                e.memset(cmask[64:128, 0, 0:64], 1.0)
                e.memset(cmask[64:128, 0, 64:128], 0.0)
                e.memset(cmask[64:128, 1, 0:64], 0.0)
                e.memset(cmask[64:128, 1, 64:128], 1.0)
                e.memset(A_VI[:, :, 64:65], 1.0)
                e.memset(id_bf[:], 0.0)
                e.memset(id_f32[:], 0.0)
                return e.memset(A_VI[:, :, 129:130], 1.0)
            c.op("gpsimd", base, incs=[("cst", 1)])
            c.op("gpsimd", lambda e: e.affine_select(
                out=id_bf[:], in_=id_bf[:], compare_op=ALU.not_equal,
                fill=1.0, base=0, pattern=[[-1, P]], channel_multiplier=1),
                waits=[("cst", c.c("cst"))], incs=[("cst", 1)])
            c.op("gpsimd", lambda e: e.affine_select(
                out=id_f32[:], in_=id_f32[:], compare_op=ALU.not_equal,
                fill=1.0, base=0, pattern=[[-1, P]], channel_multiplier=1),
                waits=[("cst", c.c("cst"))], incs=[("cst", 1)])
            for m in range(4):
                c.op("gpsimd", lambda e: e.memset(sqs[:, 0:512], 1.0),
                     waits=[("ms", c.c("ms"))], incs=[("cst", 1)])
                c.op("gpsimd", lambda e, m=m: e.affine_select(
                    out=sqs[:, 0:512], in_=sqs[:, 0:512], compare_op=ALU.is_ge,
                    fill=0.0, base=-128 * m, pattern=[[1, 512]],
                    channel_multiplier=-1),
                    waits=[("cst", c.c("cst"))], incs=[("cst", 1)])
                c.op("vector", lambda e, m=m: e.tensor_copy(mask01[:, m], sqs[:, 0:512]),
                     waits=[("cst", c.c("cst"))], incs=[("ms", 1)])
            c.op("sync", lambda e: e.dma_start(out=bias_sb[:], in_=bias_h),
                 incs=[("ld", 16)])
            c.op("gpsimd", lambda e: e.dma_start(out=gw_sb[:], in_=gw_h),
                 waits=[("ms", c.c("ms"))], incs=[("wl", 16)])

        def emit_ln(c, x_ap, par, xwaits, dest_fm_ap=None, pst=None):
            st = lambda i: stat[:, par, i:i + 1]
            c.op("vector", lambda e: e.reduce_sum(st(0), x_ap, axis=AX),
                 waits=list(xwaits) + [("lns", c.c("lns"))], incs=[("lnv", 1)])
            c.op("scalar", lambda e: e.activation(out=sqs[:], in_=x_ap,
                                                  func=AF.Square),
                 waits=list(xwaits) + [("lnv", c.c("lnv"))], incs=[("lns", 1)])
            c.op("vector", lambda e: e.tensor_scalar_mul(st(2), st(0), 1.0 / C),
                 waits=[("lnv", c.c("lnv"))], incs=[("lnv", 1)])
            c.op("vector", lambda e: e.reduce_sum(st(1), sqs[:], axis=AX),
                 waits=[("lns", c.c("lns"))], incs=[("lnv", 1)])
            c.op("vector", lambda e: e.tensor_scalar_mul(st(3), st(1), 1.0 / C),
                 waits=[("lnv", c.c("lnv"))], incs=[("lnv", 1)])
            c.op("vector", lambda e: e.tensor_mul(st(4), st(2), st(2)),
                 waits=[("lnv", c.c("lnv")), ("lns", c.c("lns"))], incs=[("lnv", 1)])
            c.op("vector", lambda e: e.tensor_sub(st(4), st(3), st(4)),
                 waits=[("lnv", c.c("lnv"))], incs=[("lnv", 1)])
            c.op("scalar", lambda e: e.activation(out=st(5), in_=st(4), func=AF.Sqrt,
                                                  bias=consts[:, 0:1]),
                 waits=[("lnv", c.c("lnv"))], incs=[("lns", 1)])
            c.op("vector", lambda e: e.reciprocal(st(6), st(5)),
                 waits=[("lns", c.c("lns"))], incs=[("lnv", 1)])
            c.op("vector", lambda e: e.tensor_scalar(
                out=h_tm[:], in0=x_ap, scalar1=st(2), scalar2=st(6),
                op0=ALU.subtract, op1=ALU.mult),
                waits=[("lnv", c.c("lnv"))], incs=[("lnv", 1)])
            if dest_fm_ap is not None:
                hs = c.c("lnv")
                p2 = pst if pst is not None else psT[:]
                for k in range(NCI):
                    c.op("tensor", lambda e, k=k, p2=p2: e.transpose(
                        p2[:, k * P:(k + 1) * P], h_tm[:, k * P:(k + 1) * P],
                        id_bf[:]),
                        waits=([("lnv", hs), ("hfm", c.c("hfm"))] if k == 0 else []),
                        incs=([("tr", 1)] if k == NCI - 1 else []))
                c.op("vector", lambda e, p2=p2: e.tensor_copy(dest_fm_ap, p2[:, 0:C]),
                     waits=[("tr", c.c("tr"))], incs=[("hfm", 1)])

        def phase_h1(c):
            lnw = {}
            for qh in range(4):
                sl = (qh % 2) * 8
                w = ([("ms", c.c("ms"))] if qh == 0 else
                     ([] if qh == 1 else lnw[qh - 2]))
                c.op("sync", lambda e, qh=qh, sl=sl: e.dma_start(
                    out=acc_v[:, sl:sl + 8, :],
                    in_=x0[qh * 1024:(qh + 1) * 1024, :].rearrange(
                        "(tt p) c -> p tt c", p=P)),
                    waits=w, incs=[("xin2", 16)])
                for tt in range(8):
                    gt = qh * 8 + tt
                    emit_ln(c, acc_v[:, sl + tt, :], gt % 2,
                            [("xin2", c.c("xin2"))],
                            dest_fm_ap=h_fm[:, :, gt * P:(gt + 1) * P],
                            pst=(psCD[:].bitcast(BF16) if gt % 2 else None))
                lnw[qh] = [("lnv", c.c("lnv")), ("lns", c.c("lns"))]

        def colmax_scalar(c, col_ap, k, dst_ap, wv):
            c.op("vector", lambda e: e.reduce_max(norms[0:k, 94:95], col_ap,
                                                  axis=AX),
                 waits=list(wv) + [("aux", c.c("aux"))], incs=[("fin", 1)])
            c.op("tensor", lambda e: e.transpose(psS[0:1, 256:256 + k],
                                                 norms[0:k, 94:95],
                                                 id_f32[0:k, 0:k]),
                 waits=[("fin", c.c("fin"))], incs=[("aux", 1)])
            c.op("vector", lambda e: e.reduce_max(dst_ap, psS[0:1, 256:256 + k],
                                                  axis=AX),
                 waits=[("aux", c.c("aux"))], incs=[("fin", 1)])

        def bcast_col(c, src11_ap, dst_col_ap, wv):
            c.op("tensor", lambda e: e.matmul(psS[:, 264:265], allones[0:1, :],
                                              src11_ap, start=True, stop=True),
                 waits=list(wv) + [("aux2", c.c("aux2"))], incs=[("aux", 1)])
            c.op("scalar", lambda e: e.copy(dst_col_ap, psS[:, 264:265]),
                 waits=[("aux", c.c("aux"))], incs=[("aux2", 1)])

        def stage_A(c):
            for pr in range(NPAIR):
                wsrc = wattn.rearrange("(ci p) n -> p ci n", p=P)
                for j, base in enumerate((0, C, 2 * C)):
                    c.op("gpsimd", lambda e, j=j, base=base, pr=pr: e.dma_start(
                        out=wpair[:, :, j, :],
                        in_=wsrc[:, :, base + pr * P: base + (pr + 1) * P]),
                        waits=([("pv", c.c("pv")), ("aux", c.c("aux"))]
                               if (pr > 0 and j == 0) else []),
                        incs=[("wa", 16)])
                wl = c.c("wa")
                hfm_all = c.c("hfm")
                for which, dst in ((0, A_Qp), (1, A_Kp)):
                    for tc in range(8):
                        pse = psE if (tc + which * 8) % 2 == 0 else psF
                        for ci in range(NCI):
                            c.op("tensor", lambda e, ci=ci, tc=tc, which=which, pse=pse: e.matmul(
                                pse[:], wpair[:, ci, which, :],
                                h_fm[:, ci, tc * 512:(tc + 1) * 512],
                                start=(ci == 0), stop=(ci == NCI - 1)),
                                waits=([("wa", wl), ("hfm", hfm_all),
                                        ("prj", max(0, c.c("prj") - 2)),
                                        ("prjv", max(0, c.c("prjv") - 1))]
                                       if ci == 0 else []),
                                incs=([("sc", 1)] if ci == NCI - 1 else []))
                        c.op("scalar", lambda e, tc=tc, dst=dst, pse=pse: e.activation(
                            out=dst[:, tc * 512:(tc + 1) * 512], in_=pse[:],
                            func=AF.Copy), waits=[("sc", c.c("sc"))],
                            incs=[("prj", 1)])
                        c.op("scalar", lambda e, pse=pse: e.activation(
                            out=A_sq[:, :], in_=pse[:], func=AF.Square),
                            waits=[("aux", c.c("aux"))], incs=[("prj", 1)])
                        c.op("tensor", lambda e: e.matmul(
                            psS[0:2, 0:512], ones2[:], A_sq[:, :],
                            start=True, stop=True),
                            waits=[("prj", c.c("prj")), ("prjv", c.c("prjv"))],
                            incs=[("aux", 1)])
                        slot = which * 8 + tc
                        c.op("vector", lambda e, slot=slot: e.reduce_max(
                            norms[0:2, 64 + slot:65 + slot], psS[0:2, 0:512], axis=AX),
                            waits=[("aux", c.c("aux"))], incs=[("prjv", 1)])
                c.op("vector", lambda e: e.reduce_max(norms[0:2, 80:81],
                                                      norms[0:2, 64:80], axis=AX),
                     waits=[("prjv", c.c("prjv"))], incs=[("fin", 1)])
                c.op("tensor", lambda e: e.transpose(psS[0:1, 256:258],
                                                     norms[0:2, 80:81],
                                                     id_f32[0:2, 0:2]),
                     waits=[("fin", c.c("fin"))], incs=[("aux", 1)])
                c.op("vector", lambda e: e.reduce_max(smalls[0:1, 0:1],
                                                      psS[0:1, 256:258], axis=AX),
                     waits=[("aux", c.c("aux"))], incs=[("fin", 1)])
                c.op("vector", lambda e: e.tensor_scalar_mul(smalls[0:1, 3:4],
                                                             smalls[0:1, 0:1],
                                                             -0.125),
                     waits=[("fin", c.c("fin"))], incs=[("fin", 1)])
                bcast_col(c, smalls[0:1, 3:4], smalls[:, 4:5],
                          [("fin", c.c("fin")), ("ex", c.c("ex"))])
                for tt in range(NT):
                    pse = psE if tt % 2 == 0 else psF
                    for ci in range(NCI):
                        c.op("tensor", lambda e, tt=tt, ci=ci, pse=pse: e.matmul(
                            pse[:, 0:128], h_fm[:, ci, tt * P:(tt + 1) * P],
                            wpair[:, ci, 2, :],
                            start=(ci == 0), stop=(ci == NCI - 1)),
                            waits=([("prjv", max(0, c.c("prjv") - 1)),
                                    ("prj", c.c("prj"))]
                                   if ci == 0 else []),
                            incs=([("sc", 1)] if ci == NCI - 1 else []))
                    c.op("vector", lambda e, tt=tt, pse=pse: e.tensor_copy(
                        A_VI[:, tt, 0:64], pse[:, 0:64]),
                        waits=[("sc", c.c("sc"))], incs=[])
                    c.op("vector", lambda e, tt=tt, pse=pse: e.tensor_copy(
                        A_VI[:, tt, 65:129], pse[:, 64:128]),
                        incs=[("prjv", 1)])
                prod_v = c.c("prjv")
                if DEBUG_DUMPS and pr == 0:
                    c.op("sync", lambda e: e.dma_start(out=dbg_q, in_=A_Qp),
                         waits=[("prjv", prod_v), ("prj", c.c("prj")),
                                ("aux2", c.c("aux2"))],
                         incs=[("out", 16)])
                    c.op("sync", lambda e: e.dma_start(out=dbg_k, in_=A_Kp),
                         incs=[("out", 16)])
                    c.op("sync", lambda e: e.dma_start(
                        out=dbg_vi, in_=A_VI.rearrange("p a b -> p (a b)")),
                        incs=[("out", 16)])
                    c.op("sync", lambda e: e.dma_start(out=dbg_sm, in_=smalls[:]),
                         incs=[("out", 16)])
                    c.op("sync", lambda e: e.dma_start(
                        out=dbg_h, in_=h_fm[:, 0, :]),
                        incs=[("out", 16)])
                def pts(kc):
                    b = kc % 4
                    buf = A_pt if b < 2 else A_pt2
                    return buf[:, (b % 2) * 1024:(b % 2 + 1) * 1024]

                def emit_qk(kc, q_sl, first):
                    sc_ps = psAB if kc % 2 == 0 else psCD
                    c.op("tensor", lambda e, kc=kc, sc_ps=sc_ps, q_sl=q_sl: e.matmul(
                        sc_ps[:, 0:512], A_Kp[0:64, kc * P:(kc + 1) * P],
                        A_Qp[0:64, q_sl], start=True, stop=True),
                        waits=([("ex", max(0, c.c("ex") - 1))] +
                               ([("prjv", prod_v), ("aux2", c.c("aux2")),
                                 ("xin", c.c("xin")), ("lns", c.c("lns")),
                                 ("lnv", c.c("lnv"))]
                                if first else [])),
                        incs=[])
                    c.op("tensor", lambda e, kc=kc, sc_ps=sc_ps, q_sl=q_sl: e.matmul(
                        sc_ps[:, 512:1024], A_Kp[64:128, kc * P:(kc + 1) * P],
                        A_Qp[64:128, q_sl], start=True, stop=True,
                        tile_position=(64, 0)),
                        incs=[("sc", 1)])

                for qc in range(8):
                    q_sl = slice(qc * 512, (qc + 1) * 512)
                    nkc = 4 * (qc + 1)
                    emit_qk(0, q_sl, qc == 0)
                    for kc in range(nkc):
                        sc_ps = psAB if kc % 2 == 0 else psCD
                        diag = kc >= 4 * qc
                        pt = pts(kc)
                        c.op("scalar", lambda e, sc_ps=sc_ps, pt=pt: e.activation(
                            out=pt, in_=sc_ps[:], func=AF.Exp, scale=0.125,
                            bias=smalls[:, 4:5]),
                            waits=[("sc", c.c("sc")),
                                   ("pv", max(0, c.c("pv") - 3))],
                            incs=[("ex", 1)])
                        if diag:
                            m = kc - 4 * qc
                            c.op("vector", lambda e, m=m, pt=pt: e.tensor_mul(
                                pt[:, 0:512], pt[:, 0:512], mask01[:, m]),
                                waits=[("ex", c.c("ex"))], incs=[])
                            c.op("vector", lambda e, m=m, pt=pt: e.tensor_mul(
                                pt[:, 512:1024], pt[:, 512:1024], mask01[:, m]),
                                incs=[("ms", 1)])
                        if kc + 1 < nkc:
                            emit_qk(kc + 1, q_sl, False)
                        pvw = ([("ms", c.c("ms"))] if diag else
                               [("ex", c.c("ex"))])
                        c.op("tensor", lambda e, kc=kc, pt=pt, nkc=nkc: e.matmul(
                            psE[0:65, :], A_VI[:, kc, 0:65], pt[:, 0:512],
                            start=(kc == 0), stop=(kc == nkc - 1)),
                            waits=pvw + ([("fin", c.c("fin")),
                                          ("aof", c.c("aof"))] if kc == 0 else []),
                            incs=[])
                        c.op("tensor", lambda e, kc=kc, pt=pt, nkc=nkc: e.matmul(
                            psF[0:65, :], A_VI[:, kc, 65:130], pt[:, 512:1024],
                            start=(kc == 0), stop=(kc == nkc - 1)),
                            incs=[("pv", 1)])
                    c.op("vector", lambda e: e.reciprocal(lrow[64:65, 0:512],
                                                          psE[64:65, :]),
                         waits=[("pv", c.c("pv"))], incs=[("fin", 1)])
                    c.op("vector", lambda e: e.reciprocal(lrow[64:65, 512:1024],
                                                          psF[64:65, :]),
                         incs=[("fin", 1)])
                    c.op("tensor", lambda e: e.matmul(psS[:, 0:512],
                                                      cmask[64:65, 0, :],
                                                      lrow[64:65, 0:512],
                                                      start=True, stop=False),
                         waits=[("fin", c.c("fin")), ("aux2", c.c("aux2"))],
                         incs=[])
                    c.op("tensor", lambda e: e.matmul(psS[:, 0:512],
                                                      cmask[64:65, 1, :],
                                                      lrow[64:65, 512:1024],
                                                      start=False, stop=True),
                         incs=[("aux", 1)])
                    c.op("scalar", lambda e: e.copy(rlbuf[:], psS[:, 0:512]),
                         waits=[("aux", c.c("aux"))], incs=[("aux2", 1)])
                    c.op("vector", lambda e, pr=pr, q_sl=q_sl: e.tensor_tensor(
                        ao_fm[0:64, pr, q_sl], psE[0:64, :], rlbuf[0:64, :],
                        ALU.mult),
                        waits=[("aux2", c.c("aux2"))], incs=[("fin", 1)])
                    c.op("vector", lambda e, pr=pr, q_sl=q_sl: e.tensor_tensor(
                        ao_fm[64:128, pr, q_sl], psF[0:64, :], rlbuf[64:128, :],
                        ALU.mult),
                        incs=[("aof", 1)])
            aof_all = c.c("aof")
            if DEBUG_DUMPS:
                c.op("sync", lambda e: e.dma_start(
                    out=dbg_ao, in_=ao_fm[:, 0, :]),
                    waits=[("aof", aof_all)], incs=[("out", 16)])
            c.op("gpsimd", lambda e: e.dma_start(
                out=A_wproj[:], in_=wproj.rearrange("(ci p) j -> p ci j", p=P)),
                waits=[("pv", c.c("pv")), ("aux", c.c("aux"))], incs=[("wa", 16)])
            wpl = c.c("wa")
            for tt in range(NT):
                par = tt % 2
                c.op("sync", lambda e, tt=tt: e.dma_start(
                    out=x_tile[:], in_=x0[tt * P:(tt + 1) * P, :]),
                    waits=[("aof", aof_all), ("res", c.c("res")),
                           ("lnv", c.c("lnv")), ("lns", c.c("lns")),
                           ("xst", c.c("xst")), ("ex", c.c("ex")),
                           ("ms", c.c("ms"))],
                    incs=[("xin", 16)])
                xinw = c.c("xin")
                for jh in range(2):
                    pse = psE if jh == 0 else psF
                    for ci in range(NCI):
                        c.op("tensor", lambda e, tt=tt, jh=jh, ci=ci, pse=pse: e.matmul(
                            pse[:], ao_fm[:, ci, tt * P:(tt + 1) * P],
                            A_wproj[:, ci, jh * 512:(jh + 1) * 512],
                            start=(ci == 0), stop=(ci == NCI - 1)),
                            waits=([("aof", aof_all), ("wa", wpl),
                                    ("res", max(0, c.c("res") - 1)),
                                    ("fin", c.c("fin"))]
                                   if ci == 0 else []),
                            incs=([("yev", 1)] if ci == NCI - 1 else []))
                    c.op("vector", lambda e, jh=jh, pse=pse: e.tensor_add(
                        x_tile[:, jh * 512:(jh + 1) * 512],
                        x_tile[:, jh * 512:(jh + 1) * 512], pse[:]),
                        waits=[("yev", c.c("yev")), ("xin", xinw)],
                        incs=[("res", 1)])
                c.op("sync", lambda e, tt=tt: e.dma_start(
                    out=xA[tt * P:(tt + 1) * P, :], in_=x_tile[:]),
                    waits=[("res", c.c("res"))], incs=[("xst", 16)])
                if "B" in STAGES:
                    emit_ln(c, x_tile[:], par, [("res", c.c("res"))],
                            dest_fm_ap=h_fm[:, :, tt * P:(tt + 1) * P],
                            pst=(psCD[:].bitcast(BF16) if tt % 2 else None))

        def stage_ffn(c, which):
            full = which == "B"
            ntile = NT if full else NTH
            hsrc = h_fm if full else h4_fm
            xsrc = xA if full else xC
            if not full:
                for qh in range(2):
                    c.op("sync", lambda e, qh=qh: e.dma_start(
                        out=acc_v[:, qh * 8:(qh + 1) * 8, :],
                        in_=xC[qh * 1024:(qh + 1) * 1024, :].rearrange(
                            "(tt p) c -> p tt c", p=P)),
                        waits=([("acc", c.c("acc")), ("xst", c.c("xst"))]
                               if qh == 0 else []),
                        incs=[("xin2", 16)])
                    for tt in range(8):
                        gt = qh * 8 + tt
                        emit_ln(c, acc_v[:, gt, :], gt % 2,
                                [("xin2", c.c("xin2"))],
                                dest_fm_ap=h4_fm[:, :, gt * P:(gt + 1) * P],
                                pst=(psCD[:].bitcast(BF16) if gt % 2 else None))
            hfm_need = c.c("hfm")
            for tt in range(ntile):
                for ci in range(NCI):
                    c.op("tensor", lambda e, tt=tt, ci=ci: e.matmul(
                        psS[:, 0:NE], hsrc[:, ci, tt * P:(tt + 1) * P],
                        gw_sb[:, ci, :],
                        start=(ci == 0), stop=(ci == NCI - 1)),
                        waits=([("hfm", hfm_need), ("wl", 16),
                                ("rt", c.c("rt"))] if ci == 0 else []),
                        incs=([("sc", 1)] if ci == NCI - 1 else []))
                c.op("vector", lambda e: e.reduce_max(stat[:, 0, 0:1], psS[:, 0:NE],
                                                      axis=AX),
                     waits=[("sc", c.c("sc"))], incs=[("rt", 1)])
                c.op("vector", lambda e: e.tensor_scalar_mul(stat[:, 0, 1:2],
                                                             stat[:, 0, 0:1], -1.0),
                     waits=[("rt", c.c("rt"))], incs=[("rt", 1)])
                c.op("scalar", lambda e: e.activation(out=probs[:], in_=psS[:, 0:NE],
                                                      func=AF.Exp,
                                                      bias=stat[:, 0, 1:2]),
                     waits=[("rt", c.c("rt"))], incs=[("rt", 1)])
                c.op("vector", lambda e: e.max(out=m8[:], in_=probs[:]),
                     waits=[("rt", c.c("rt"))], incs=[("rt", 1)])
                c.op("vector", lambda e: e.tensor_scalar(pm[:], probs[:],
                                                         m8[:, 7:8], None,
                                                         op0=ALU.is_ge),
                     waits=[("rt", c.c("rt"))], incs=[("rt", 1)])
                c.op("vector", lambda e: e.tensor_mul(pm[:], pm[:], probs[:]),
                     waits=[("rt", c.c("rt"))], incs=[("rt", 1)])
                c.op("vector", lambda e: e.reduce_sum(stat[:, 0, 2:3], pm[:], axis=AX),
                     waits=[("rt", c.c("rt"))], incs=[("rt", 1)])
                c.op("vector", lambda e: e.reciprocal(stat[:, 0, 3:4], stat[:, 0, 2:3]),
                     waits=[("rt", c.c("rt"))], incs=[("rt", 1)])
                c.op("vector", lambda e, tt=tt: e.tensor_scalar_mul(
                    comb[:, tt, :], pm[:], stat[:, 0, 3:4]),
                    waits=[("rt", c.c("rt"))], incs=[("rt", 1)])
            rt_done = c.c("rt")
            nhalf = 2 if full else 1
            for hf in range(nhalf):
                tok0 = hf * TH
                if full:
                    c.op("sync", lambda e, tok0=tok0: e.dma_start(
                        out=acc_v[:], in_=xsrc[tok0:tok0 + TH, :].rearrange(
                            "(tt p) c -> p tt c", p=P)),
                        waits=[("acc", c.c("acc")), ("xst", c.c("xst"))],
                        incs=[("xin2", 16)])
                accw = c.c("xin2")
                for e_i in range(NE):
                    par = e_i % 2
                    c.op("gpsimd", lambda e, e_i=e_i, par=par: e.dma_start(
                        out=B_guw[:, par], in_=guwT[e_i].rearrange(
                            "(ci p) j -> p ci j", p=P)),
                        waits=[("wf", max(0, c.c("wf") - 2))],
                        incs=[("wl", 16)])
                    guw_rdy = c.c("wl")
                    c.op("gpsimd", lambda e, e_i=e_i: e.dma_start(
                        out=B_dw[:], in_=dwT[e_i].rearrange(
                            "(ci p) j -> p ci j", p=P)),
                        waits=[("dn", c.c("dn"))],
                        incs=[("wl", 16)])
                    dw_rdy = c.c("wl")
                    for tc in range(4):
                        tsl = slice(tok0 + tc * 512, tok0 + (tc + 1) * 512)
                        for guj in range(8):
                            bank = [psAB[:, 0:512], psAB[:, 512:1024],
                                    psCD[:, 0:512], psCD[:, 512:1024]][guj % 4]
                            bw = ([("guv", c.c("guv"))] if guj < 4
                                  else [("gus", c.c("gus"))])
                            for ci in range(NCI):
                                c.op("tensor", lambda e, guj=guj, ci=ci, bank=bank, tsl=tsl, par=par: e.matmul(
                                    bank, B_guw[:, par, ci, guj * P:(guj + 1) * P],
                                    hsrc[:, ci, tsl],
                                    start=(ci == 0), stop=(ci == NCI - 1)),
                                    waits=(([("wl", guw_rdy)] + bw) if ci == 0 else []),
                                    incs=([("gu", 1)] if ci == NCI - 1 else []))
                            if guj < 4:
                                c.op("scalar", lambda e, guj=guj, bank=bank: e.activation(
                                    out=B_sg[:, guj, :], in_=bank, func=AF.Silu),
                                    waits=[("gu", c.c("gu")),
                                           ("guv", c.c("guv"))],
                                    incs=[("gus", 1)])
                            else:
                                c.op("vector", lambda e, guj=guj, bank=bank: e.tensor_tensor(
                                    out=B_t[:, guj - 4, :], in0=bank,
                                    in1=B_sg[:, guj - 4, :], op=ALU.mult),
                                    waits=[("gu", c.c("gu")),
                                           ("gus", c.c("gus")),
                                           ("dn", c.c("dn")),
                                           ("ms", c.c("ms"))],
                                    incs=[("guv", 1)])
                        guv_now = c.c("guv")
                        for st4 in range(4):
                            gt = tok0 // P + tc * 4 + st4
                            for jh in range(2):
                                pse = psE if (st4 * 2 + jh) % 2 == 0 else psF
                                for ich in range(4):
                                    c.op("tensor", lambda e, st4=st4, jh=jh, ich=ich, pse=pse: e.matmul(
                                        pse[:], B_t[:, ich, st4 * P:(st4 + 1) * P],
                                        B_dw[:, ich, jh * 512:(jh + 1) * 512],
                                        start=(ich == 0), stop=(ich == 3)),
                                        waits=([("guv", guv_now), ("wl", dw_rdy),
                                                ("acc", max(0, c.c("acc") - 1))]
                                               if ich == 0 else []),
                                        incs=([("dn", 1)] if ich == 3 else []))
                                c.op("vector", lambda e, gt=gt, jh=jh, pse=pse, e_i=e_i, hf=hf: e.scalar_tensor_tensor(
                                    out=acc_v[:, gt - hf * NTH, jh * 512:(jh + 1) * 512],
                                    in0=pse[:], scalar=comb[:, gt, e_i:e_i + 1],
                                    in1=acc_v[:, gt - hf * NTH, jh * 512:(jh + 1) * 512],
                                    op0=ALU.mult, op1=ALU.add),
                                    waits=[("dn", c.c("dn")), ("xin2", accw),
                                           ("rt", rt_done)],
                                    incs=[("acc", 1)])
                    c.op("tensor", lambda e: e.nop(), incs=[("wf", 1)])
                acc_done = c.c("acc")
                if full:
                    for tt in range(NTH):
                        c.op("sync", lambda e, tt=tt, tok0=tok0: e.dma_start(
                            out=xB[tok0 + tt * P: tok0 + (tt + 1) * P, :],
                            in_=acc_v[:, tt, :]),
                            waits=[("acc", acc_done)], incs=[("xst", 16)])
                else:
                    for tt in range(NTH):
                        c.op("sync", lambda e, tt=tt: e.dma_start(
                            out=y_out[tt * P:(tt + 1) * P, :], in_=acc_v[:, tt, :]),
                            waits=[("acc", acc_done)], incs=[("out", 16)])

        def stage_C(c):
            c.op("sync", lambda e: e.dma_start(
                out=xQ[:, :], in_=roff(c, xB[0:TH, :])),
                waits=[("xst", c.c("xst"))], incs=[("xin2", 16)])
            xq_rdy = c.c("xin2")
            if C_SUB == "xq":
                return
            c.op("gpsimd", lambda e: e.dma_start(
                out=C_wk[:, :, :], in_=wkT.rearrange("(ci p) j -> p ci j", p=P)),
                waits=[("acc", c.c("acc"))], incs=[("wa", 16)])
            c.op("gpsimd", lambda e: e.dma_start(
                out=C_wv[:, :, :], in_=wvT.rearrange("(ci p) j -> p ci j", p=P)),
                incs=[("wa", 16)])
            wrdy = c.c("wa")
            for jh in range(2):
                c.op("sync", lambda e, jh=jh: e.dma_start(
                    out=rowslot[:], in_=brow_h[2:3, jh * 512:(jh + 1) * 512]),
                    waits=[("aux", c.c("aux")), ("xin2", c.c("xin2"))],
                    incs=[("ld", 16)])
                c.op("tensor", lambda e: e.matmul(
                    psS[:, 0:512], allones[0:1, :], rowslot[:],
                    start=True, stop=True),
                    waits=[("ld", c.c("ld")), ("aux2", c.c("aux2")),
                           ("kv", c.c("kv")), ("ch", c.c("ch")),
                           ("fin", c.c("fin"))],
                    incs=[("aux", 1)])
                c.op("scalar", lambda e, jh=jh: e.copy(
                    C_bvb[:, jh * 512:(jh + 1) * 512], psS[:, 0:512]),
                    waits=[("aux", c.c("aux"))], incs=[("aux2", 1)])
            bvb_rdy = c.c("aux2")
            for tc in range(8):
                for st4 in range(4):
                    tt = tc * 4 + st4
                    par = tt % 2
                    c.op("sync", lambda e, tt=tt: e.dma_start(
                        out=x_tile[:], in_=xB[tt * P:(tt + 1) * P, :]),
                        waits=[("lnv", c.c("lnv")), ("lns", c.c("lns")),
                               ("xst", c.c("xst"))],
                        incs=[("xin", 16)])
                    emit_ln(c, x_tile[:], par, [("xin", c.c("xin"))],
                            dest_fm_ap=hq_v[:, :, st4 * P:(st4 + 1) * P])
                hq_rdy = c.c("hfm")
                for jch in range(8):
                    pse = psE if jch % 2 == 0 else psF
                    for ci in range(NCI):
                        c.op("tensor", lambda e, jch=jch, ci=ci, pse=pse: e.matmul(
                            pse[:], C_wk[:, ci, jch * P:(jch + 1) * P],
                            hq_v[:, ci, :],
                            start=(ci == 0), stop=(ci == NCI - 1)),
                            waits=([("hfm", hq_rdy), ("wa", wrdy),
                                    ("kv", max(0, c.c("kv") - 1)),
                                    ("ksq", max(0, c.c("ksq") - 1))]
                                   if ci == 0 else []),
                            incs=([("sc", 1)] if ci == NCI - 1 else []))
                    c.op("vector", lambda e, jch=jch, tc=tc, pse=pse: e.tensor_scalar(
                        out=K_fm[:, jch, tc * 512:(tc + 1) * 512],
                        in0=pse[:], scalar1=bias_sb[:, 1, jch:jch + 1],
                        scalar2=None, op0=ALU.add),
                        waits=[("sc", c.c("sc")), ("ld", 16)],
                        incs=[("kv", 1)])
                    c.op("scalar", lambda e, jch=jch, tc=tc: e.activation(
                        out=rlbuf[:].bitcast(BF16)[:, 0:512],
                        in_=K_fm[:, jch, tc * 512:(tc + 1) * 512],
                        func=AF.Square),
                        waits=[("qn", c.c("qn")), ("kv", c.c("kv"))],
                        incs=[("ksq", 1)])
                    c.op("tensor", lambda e, jch=jch: e.matmul(
                        psS[0:1, 0:512], onesr_b[:],
                        rlbuf[:].bitcast(BF16)[:, 0:512],
                        start=(jch == 0), stop=(jch == 7)),
                        waits=[("ksq", c.c("ksq"))] +
                              ([("ch", c.c("ch")), ("fin", c.c("fin"))]
                               if jch == 0 else []),
                        incs=[("qn", 1)])
                c.op("vector", lambda e: e.tensor_copy(
                    rowslot2[:], psS[0:1, 0:512]),
                    waits=[("qn", c.c("qn"))], incs=[("ch", 1)])
                for q4 in range(4):
                    c.op("tensor", lambda e, q4=q4: e.transpose(
                        psS[:, 256:257],
                        rowslot2[0:1, q4 * P:(q4 + 1) * P],
                        id_f32[0:1, 0:1]),
                        waits=[("ch", c.c("ch")), ("fin", c.c("fin"))],
                        incs=[("aux", 1)])
                    col = tc * 4 + q4
                    c.op("vector", lambda e, col=col: e.tensor_copy(
                        norms[:, col:col + 1], psS[:, 256:257]),
                        waits=[("aux", c.c("aux"))], incs=[("fin", 1)])
                for st4 in range(4):
                    tt = tc * 4 + st4
                    for jh in range(2):
                        pse = psE if (st4 * 2 + jh) % 2 == 0 else psF
                        for ci in range(NCI):
                            c.op("tensor", lambda e, st4=st4, jh=jh, ci=ci, pse=pse: e.matmul(
                                pse[:], hq_v[:, ci, st4 * P:(st4 + 1) * P],
                                C_wv[:, ci, jh * 512:(jh + 1) * 512],
                                start=(ci == 0), stop=(ci == NCI - 1)),
                                waits=([("kv", max(0, c.c("kv") - 1))]
                                       if ci == 0 else []),
                                incs=([("sc", 1)] if ci == NCI - 1 else []))
                        c.op("vector", lambda e, tt=tt, jh=jh, pse=pse: e.tensor_add(
                            V_tm[:, tt, jh * 512:(jh + 1) * 512], pse[:],
                            C_bvb[:, jh * 512:(jh + 1) * 512]),
                            waits=[("sc", c.c("sc")), ("aux2", bvb_rdy)],
                            incs=[("kv", 1)])
            c.op("scalar", lambda e: e.activation(out=norms[:, 32:64],
                                                  in_=norms[:, 0:32], func=AF.Sqrt,
                                                  bias=consts[:, 2:3]),
                 waits=[("fin", c.c("fin"))], incs=[("chs", 1)])
            c.op("vector", lambda e: e.tensor_scalar_add(norms[:, 32:64],
                                                         norms[:, 32:64], 1e-8),
                 waits=[("chs", c.c("chs"))], incs=[("ch", 1)])
            c.op("vector", lambda e: e.reciprocal(norms[:, 32:64], norms[:, 32:64]),
                 waits=[("ch", c.c("ch"))], incs=[("ch", 1)])
            c.op("vector", lambda e: e.tensor_scalar_mul(norms[:, 32:64],
                                                         norms[:, 32:64], PI),
                 waits=[("ch", c.c("ch"))], incs=[("ch", 1)])
            colmax_scalar(c, norms[:, 0:32], P, smalls[0:1, 8:9],
                          [("ch", c.c("ch"))])
            c.op("gpsimd", lambda e: e.dma_start(
                out=C_wsl[:, :, :],
                in_=wqT.rearrange("(ci p) j -> p ci j", p=P)[:, :, 0:512]),
                waits=[("acc", c.c("acc"))], incs=[("wa", 16)])
            c.op("gpsimd", lambda e: e.dma_start(
                out=C_wq2[:, :, :],
                in_=wqT.rearrange("(ci p) j -> p ci j", p=P)[:, :, 512:1024]),
                waits=[("ms", c.c("ms")), ("acc", c.c("acc"))],
                incs=[("wa", 16)])
            wq_rdy = c.c("wa")
            for tc in range(4):
                for st4 in range(4):
                    tt = tc * 4 + st4
                    par = tt % 2
                    c.op("sync", lambda e, tt=tt: e.dma_start(
                        out=x_tile[:],
                        in_=xQ[tt * P:(tt + 1) * P, :]),
                        waits=[("lnv", c.c("lnv")), ("lns", c.c("lns")),
                               ("xin2", xq_rdy)],
                        incs=[("xin", 16)])
                    emit_ln(c, x_tile[:], par, [("xin", c.c("xin"))],
                            dest_fm_ap=hq_v[:, :, st4 * P:(st4 + 1) * P])
                hq_rdy = c.c("hfm")
                for jch in range(8):
                    wsrc = C_wsl if jch < 4 else C_wq2
                    jj = jch % 4
                    pse = psE if jch % 2 == 0 else psF
                    for ci in range(NCI):
                        c.op("tensor", lambda e, jj=jj, ci=ci, pse=pse, wsrc=wsrc: e.matmul(
                            pse[:], wsrc[:, ci, jj * P:(jj + 1) * P],
                            hq_v[:, ci, :],
                            start=(ci == 0), stop=(ci == NCI - 1)),
                            waits=([("hfm", hq_rdy), ("wa", wq_rdy),
                                    ("kv", max(0, c.c("kv") - 1)),
                                    ("ksq", max(0, c.c("ksq") - 1))]
                                   if ci == 0 else []),
                            incs=([("sc", 1)] if ci == NCI - 1 else []))
                    c.op("vector", lambda e, jch=jch, tc=tc, pse=pse: e.tensor_scalar(
                        out=C_Q[:, jch, tc * 512:(tc + 1) * 512],
                        in0=pse[:], scalar1=bias_sb[:, 0, jch:jch + 1],
                        scalar2=None, op0=ALU.add),
                        waits=[("sc", c.c("sc"))], incs=[("kv", 1)])
                    c.op("scalar", lambda e, jch=jch, tc=tc: e.activation(
                        out=rlbuf[:].bitcast(BF16)[:, 0:512],
                        in_=C_Q[:, jch, tc * 512:(tc + 1) * 512],
                        func=AF.Square),
                        waits=[("qn", c.c("qn")), ("kv", c.c("kv"))],
                        incs=[("ksq", 1)])
                    c.op("tensor", lambda e, jch=jch: e.matmul(
                        psS[0:1, 0:512], onesr_b[:],
                        rlbuf[:].bitcast(BF16)[:, 0:512],
                        start=(jch == 0), stop=(jch == 7)),
                        waits=[("ksq", c.c("ksq"))] +
                              ([("ch", c.c("ch")), ("fin", c.c("fin"))]
                               if jch == 0 else []),
                        incs=[("qn", 1)])
                c.op("vector", lambda e: e.tensor_copy(rowslot2[:],
                                                       psS[0:1, 0:512]),
                     waits=[("qn", c.c("qn"))], incs=[("ch", 1)])
                for q4 in range(4):
                    c.op("tensor", lambda e, q4=q4: e.transpose(
                        psS[:, 256:257], rowslot2[0:1, q4 * P:(q4 + 1) * P],
                        id_f32[0:1, 0:1]),
                        waits=[("ch", c.c("ch")), ("fin", c.c("fin"))],
                        incs=[("aux", 1)])
                    col = 64 + tc * 4 + q4
                    c.op("vector", lambda e, col=col: e.tensor_copy(
                        norms[:, col:col + 1], psS[:, 256:257]),
                        waits=[("aux", c.c("aux"))], incs=[("fin", 1)])
            q_done = c.c("kv")
            if C_SUB == "q":
                return
            c.op("scalar", lambda e: e.activation(out=norms[:, 80:96],
                                                  in_=norms[:, 64:80], func=AF.Sqrt,
                                                  bias=consts[:, 2:3]),
                 waits=[("fin", c.c("fin"))], incs=[("chs", 1)])
            c.op("vector", lambda e: e.tensor_scalar_add(norms[:, 80:96],
                                                         norms[:, 80:96], 1e-8),
                 waits=[("chs", c.c("chs"))], incs=[("ch", 1)])
            c.op("vector", lambda e: e.reciprocal(norms[:, 80:96], norms[:, 80:96]),
                 waits=[("ch", c.c("ch"))], incs=[("ch", 1)])
            colmax_scalar(c, norms[:, 64:80], P, smalls[0:1, 9:10],
                          [("ch", c.c("ch"))])
            c.op("vector", lambda e: e.tensor_mul(smalls[0:1, 10:11],
                                                  smalls[0:1, 8:9],
                                                  smalls[0:1, 9:10]),
                 waits=[("fin", c.c("fin"))], incs=[("fin", 1)])
            c.op("scalar", lambda e: e.activation(out=smalls[0:1, 11:12],
                                                  in_=smalls[0:1, 10:11],
                                                  func=AF.Sqrt,
                                                  bias=consts[0:1, 2:3]),
                 waits=[("fin", c.c("fin"))], incs=[("chs", 1)])
            c.op("vector", lambda e: e.tensor_scalar_mul(smalls[0:1, 12:13],
                                                         smalls[0:1, 11:12],
                                                         -1.0 / 32.0),
                 waits=[("chs", c.c("chs"))], incs=[("fin", 1)])
            bcast_col(c, smalls[0:1, 12:13], smalls[:, 13:14],
                      [("fin", c.c("fin"))])
            for tq in range(NTH):
                c.op("vector", lambda e, tq=tq: e.tensor_scalar(
                    sqs[:, 0:P], id_f32[:], norms[:, 80 + tq:81 + tq], None,
                    op0=ALU.mult),
                    waits=[("ch", c.c("ch")), ("lns", c.c("lns")),
                           ("aux", c.c("aux"))],
                    incs=[("ch", 1)])
                c.op("tensor", lambda e, tq=tq: e.matmul(
                    psS[:, 0:128], allones[:], sqs[:, 0:P],
                    start=True, stop=True),
                    waits=[("ch", c.c("ch")), ("aux2", c.c("aux2"))],
                    incs=[("aux", 1)])
                c.op("scalar", lambda e, tq=tq: e.copy(
                    rqb[:, tq * P:(tq + 1) * P], psS[:, 0:128]),
                    waits=[("aux", c.c("aux"))], incs=[("aux2", 1)])
            rqb_done = c.c("aux2")
            for bh in range(2):
                c.op("sync", lambda e, bh=bh: e.dma_start(
                    out=rowslot[:], in_=brow_h[3:4, bh * 512:(bh + 1) * 512]),
                    waits=[("aux", c.c("aux")), ("acc", c.c("acc"))],
                    incs=[("ld", 16)])
                c.op("tensor", lambda e: e.matmul(
                    psS[:, 0:512], allones[0:1, :], rowslot[:],
                    start=True, stop=True),
                    waits=[("ld", c.c("ld")), ("aux2", c.c("aux2"))],
                    incs=[("aux", 1)])
                c.op("scalar", lambda e, bh=bh: e.copy(
                    C_bob[:, bh * 512:(bh + 1) * 512], psS[:, 0:512]),
                    waits=[("aux", c.c("aux")), ("rt", c.c("rt"))],
                    incs=[("aux2", 1)])
            bob_rdy = c.c("aux2")
            if C_SUB == "pre":
                return
            for qc in range(8):
                q_sl = slice(qc * 256, (qc + 1) * 256)
                for kc in range(NT):
                    sc_ps = psE if kc % 2 == 0 else psF
                    first = (qc == 0 and kc == 0)
                    for ci in range(NCI):
                        qkw = []
                        if ci == 0:
                            qkw = [("ch", max(0, c.c("ch") - 3))]
                            if kc <= 1:
                                qkw += [("res", c.c("res"))]
                            if first:
                                qkw += [("kv", q_done), ("aux2", bob_rdy),
                                        ("ksq", c.c("ksq"))]
                        c.op("tensor", lambda e, kc=kc, ci=ci, sc_ps=sc_ps, q_sl=q_sl: e.matmul(
                            sc_ps[:, 0:256], K_fm[:, ci, kc * P:(kc + 1) * P],
                            C_Q[:, ci, q_sl],
                            start=(ci == 0), stop=(ci == NCI - 1)),
                            waits=qkw,
                            incs=([("sc", 1)] if ci == NCI - 1 else []))
                    c.op("vector", lambda e, sc_ps=sc_ps, q_sl=q_sl: e.tensor_tensor(
                        sqs[:, 0:256], sc_ps[:, 0:256], rqb[:, q_sl], ALU.mult),
                        waits=[("sc", c.c("sc")), ("chs", c.c("chs")),
                               ("ex", c.c("ex"))],
                        incs=[("ch", 1)])
                    c.op("scalar", lambda e, kc=kc: e.activation(
                        out=sqs[:, 256:512], in_=sqs[:, 0:256], func=AF.Abs,
                        scale=norms[:, 32 + kc:33 + kc]),
                        waits=[("ch", c.c("ch"))], incs=[("chs", 1)])
                    c.op("vector", lambda e: e.tensor_scalar_min(
                        sqs[:, 256:512], sqs[:, 256:512], PI),
                        waits=[("chs", c.c("chs"))], incs=[("ch", 1)])
                    c.op("scalar", lambda e: e.activation(
                        out=sqs[:, 256:512], in_=sqs[:, 256:512], func=AF.Sin,
                        scale=-1.0, bias=consts[:, 1:2]),
                        waits=[("ch", c.c("ch"))], incs=[("chs", 1)])
                    c.op("vector", lambda e, sc_ps=sc_ps: e.scalar_tensor_tensor(
                        out=sqs[:, 0:256], in0=sc_ps[:, 0:256], scalar=1.0 / 32.0,
                        in1=sqs[:, 256:512], op0=ALU.mult, op1=ALU.mult),
                        waits=[("chs", c.c("chs"))], incs=[("ch", 1)])
                    pt = C_pt[:, (kc % 2) * 256:(kc % 2 + 1) * 256]
                    c.op("scalar", lambda e, pt=pt: e.activation(
                        out=pt, in_=sqs[:, 0:256], func=AF.Exp,
                        bias=smalls[:, 13:14]),
                        waits=[("ch", c.c("ch")),
                               ("pv", max(0, c.c("pv") - 1))],
                        incs=[("ex", 1)])
                    exn = c.c("ex")
                    for sub in range(2):
                        for jjh in range(2):
                            bank = [psAB[:, 0:512], psAB[:, 512:1024],
                                    psCD[:, 0:512], psCD[:, 512:1024]][sub * 2 + jjh]
                            pvw = []
                            if sub == 0 and jjh == 0:
                                pvw = [("ex", exn)]
                                if kc == 0:
                                    pvw += [("fin", c.c("fin")),
                                            ("chs", c.c("chs")),
                                            ("aux", c.c("aux"))]
                            c.op("tensor", lambda e, kc=kc, sub=sub, jjh=jjh, bank=bank, pt=pt: e.matmul(
                                bank, pt[:, sub * P:(sub + 1) * P],
                                V_tm[:, kc, jjh * 512:(jjh + 1) * 512],
                                start=(kc == 0), stop=(kc == NT - 1)),
                                waits=pvw, incs=[])
                    c.op("tensor", lambda e, kc=kc, pt=pt: e.matmul(
                        psS[0:1, 0:256], onesr_b[:], pt[:],
                        start=(kc == 0), stop=(kc == NT - 1)),
                        waits=([("fin", c.c("fin"))] if kc == 0 else []),
                        incs=[("pv", 1)])
                c.op("vector", lambda e: e.reciprocal(rowslot[0:1, 0:256],
                                                      psS[0:1, 0:256]),
                     waits=[("pv", c.c("pv"))],
                     incs=[("fin", 1)])
                for sub in range(2):
                    c.op("tensor", lambda e, sub=sub: e.transpose(
                        psS[:, 300 + sub:301 + sub],
                        rowslot[0:1, sub * P:(sub + 1) * P], id_f32[0:1, 0:1]),
                        waits=[("fin", c.c("fin"))], incs=[("aux", 1)])
                c.op("vector", lambda e: e.tensor_copy(norms[:, 90:92],
                                                       psS[:, 300:302]),
                     waits=[("aux", c.c("aux"))], incs=[("fin", 1)])
                for sub in range(2):
                    for jjh in range(2):
                        bank = [psAB[:, 0:512], psAB[:, 512:1024],
                                psCD[:, 0:512], psCD[:, 512:1024]][sub * 2 + jjh]
                        c.op("scalar", lambda e, sub=sub, jjh=jjh, bank=bank: e.activation(
                            out=h_tm[:, jjh * 512:(jjh + 1) * 512], in_=bank,
                            func=AF.Copy, scale=norms[:, 90 + sub:91 + sub]),
                            waits=[("fin", c.c("fin"))] +
                                  ([("tr", c.c("tr"))] if jjh == 0 else []),
                            incs=[("chs", 1)])
                    for k in range(NCI):
                        c.op("tensor", lambda e, k=k: e.transpose(
                            psT[:, k * P:(k + 1) * P], h_tm[:, k * P:(k + 1) * P],
                            id_bf[:]),
                            waits=([("chs", c.c("chs")), ("hfm", c.c("hfm"))]
                                   if k == 0 else []),
                            incs=([("tr", 1)] if k == NCI - 1 else []))
                    c.op("vector", lambda e, sub=sub: e.tensor_copy(
                        C_aoqc[:, :, sub * P:(sub + 1) * P], psT[:, 0:C]),
                        waits=[("tr", c.c("tr")), ("yev", c.c("yev")),
                               ("ms", c.c("ms"))],
                        incs=[("hfm", 1)])
                aoq_rdy = c.c("hfm")
                for jjh in range(2):
                    c.op("gpsimd", lambda e, jjh=jjh: e.dma_start(
                        out=C_wsl[:, :, :],
                        in_=woT.rearrange("(ci p) j -> p ci j", p=P)
                        [:, :, jjh * 512:(jjh + 1) * 512]),
                        waits=[("yev", c.c("yev")), ("sc", c.c("sc"))]
                              if (qc == 0 and jjh == 0) else
                              [("yev", c.c("yev"))],
                        incs=[("wa", 16)])
                    wo_rdy = c.c("wa")
                    for sub in range(2):
                        gq = qc * 2 + sub
                        c.op("sync", lambda e, gq=gq, sub=sub, jjh=jjh: e.dma_start(
                            out=x_tile[:, sub * 512:(sub + 1) * 512],
                            in_=xQ[gq * P:(gq + 1) * P,
                                   jjh * 512:(jjh + 1) * 512]),
                            waits=[("res", c.c("res")), ("ex", c.c("ex")),
                                   ("ms", c.c("ms"))],
                            incs=[("xin", 16)])
                        xw = c.c("xin")
                        pse = psE if sub == 0 else psF
                        for ci in range(NCI):
                            c.op("tensor", lambda e, sub=sub, jjh=jjh, ci=ci, pse=pse: e.matmul(
                                pse[:], C_aoqc[:, ci, sub * P:(sub + 1) * P],
                                C_wsl[:, ci, :],
                                start=(ci == 0), stop=(ci == NCI - 1)),
                                waits=([("hfm", aoq_rdy), ("wa", wo_rdy),
                                        ("res", max(0, c.c("res") - 2))]
                                       if ci == 0 else []),
                                incs=([("yev", 1)] if ci == NCI - 1 else []))
                        c.op("vector", lambda e, jjh=jjh, pse=pse: e.tensor_add(
                            rlbuf[:], pse[:], C_bob[:, jjh * 512:(jjh + 1) * 512]),
                            waits=[("yev", c.c("yev")), ("ksq", c.c("ksq")),
                                   ("xst", c.c("xst"))],
                            incs=[("res", 1)])
                        c.op("vector", lambda e, sub=sub: e.tensor_add(
                            rlbuf[:], rlbuf[:],
                            x_tile[:, sub * 512:(sub + 1) * 512]),
                            waits=[("xin", xw)],
                            incs=[("res", 1)])
                        c.op("sync", lambda e, gq=gq, jjh=jjh: e.dma_start(
                            out=xC[gq * P:(gq + 1) * P, jjh * 512:(jjh + 1) * 512],
                            in_=rlbuf[:]),
                            waits=[("res", c.c("res"))], incs=[("xst", 16)])

        def emit_tap(c):
            last = STAGES[-1]
            if last == "D":
                c.op("sync", lambda e: (e.wait_ge(c.sems["out"], c.c("out")), None)[1])
                return
            eff = last
            if last == "C" and C_SUB != "full":
                eff = "B"
            src = {"A": xA, "B": xB, "C": xC}[eff]
            for tt in range(NTH):
                if eff in ("A", "B"):
                    inap = roff(c, src[tt * P:(tt + 1) * P, :])
                else:
                    inap = src[tt * P:(tt + 1) * P, :]
                c.op("sync", lambda e, tt=tt, inap=inap: e.dma_start(
                    out=x_tile[:], in_=inap),
                    waits=[("xst", c.c("xst")), ("out", c.c("out"))],
                    incs=[("xin2", 16)])
                c.op("sync", lambda e, tt=tt: e.dma_start(
                    out=y_out[tt * P:(tt + 1) * P, :], in_=x_tile[:]),
                    waits=[("xin2", c.c("xin2"))], incs=[("out", 16)])
            c.op("sync", lambda e: (e.wait_ge(c.sems["out"], c.c("out")), None)[1])

        def program(c):
            emit_consts(c)
            phase_h1(c)
            if "A" in STAGES:
                stage_A(c)
            c.barrier()
            if "B" in STAGES:
                stage_ffn(c, "B")
                c.barrier()
            if "C" in STAGES:
                stage_C(c)
                c.barrier()
            if "D" in STAGES:
                stage_ffn(c, "D")
                c.barrier()
            emit_tap(c)

        with nc.Block() as blk:
            for engname in ENGINES:
                def make_emit(en):
                    def emit(eng):
                        ctx.start(en, eng)
                        program(ctx)
                    return emit
                getattr(blk, engname)(make_emit(engname))
    return nc


_NC_CACHE = None


def _get_nc():
    global _NC_CACHE
    if _NC_CACHE is None:
        _NC_CACHE = build_nc()
    return _NC_CACHE


def kernel(**inputs):
    f = lambda k: np.ascontiguousarray(np.asarray(inputs[k], dtype=np.float32))
    x = f("L0_attn_a_x")
    W_attn = f("L0_attn_a_W_attn")
    W_proj = f("L0_attn_a_W_proj")
    Wq, bq = f("L0_attn_b_Wq"), f("L0_attn_b_bq")
    Wk, bk = f("L0_attn_b_Wk"), f("L0_attn_b_bk")
    Wv, bv = f("L0_attn_b_Wv"), f("L0_attn_b_bv")
    Wo, bo = f("L0_attn_b_Wo"), f("L0_attn_b_bo")
    gate_w = f("L0_ffn_gate_w")
    gate_up_w = f("L0_ffn_gate_up_w")
    down_w = f("L0_ffn_down_w")

    biases = np.ascontiguousarray(np.stack([bq, bk, bv, bo]))
    bias_hc = np.ascontiguousarray(biases.reshape(4, NCI, P).transpose(2, 0, 1))
    gw_hc = np.ascontiguousarray(gate_w.T.reshape(NCI, P, NE).transpose(1, 0, 2))
    wqT = np.ascontiguousarray(Wq.T)
    wkT = np.ascontiguousarray(Wk.T)
    wvT = np.ascontiguousarray(Wv.T)
    woT = np.ascontiguousarray(Wo.T)
    guwT = np.ascontiguousarray(gate_up_w.transpose(0, 2, 1))
    dwT = np.ascontiguousarray(down_w.transpose(0, 2, 1))

    nc = _get_nc()
    in_maps = []
    for core in range(8):
        b = core // 2
        in_maps.append({
            "x0": np.ascontiguousarray(x[b]),
            "wattn": W_attn, "wproj": W_proj,
            "wqT": wqT, "wkT": wkT, "wvT": wvT, "woT": woT,
            "bias_h": bias_hc, "brow_h": biases, "gw_h": gw_hc,
            "guwT": guwT, "dwT": dwT,
        })
    res = run_bass_kernel_spmd(nc, in_maps, core_ids=list(range(8)))
    out = np.empty((4, T, C), dtype=np.float32)
    for core in range(8):
        b, h = core // 2, core % 2
        out[b, h * TH:(h + 1) * TH, :] = res.results[core]["y"]
    return out
